# version 07
# baseline (speedup 1.0000x reference)
import contextlib
import numpy as np
import concourse.bass as bass
import concourse.mybir as mybir
from concourse.bass_utils import run_bass_kernel_spmd
from concourse.bass_core_rust import IndirectOffsetOnAxis

F32 = mybir.dt.float32
BF16 = mybir.dt.bfloat16
I32 = mybir.dt.int32
U32 = mybir.dt.uint32
AF = mybir.ActivationFunctionType
ALU = mybir.AluOpType
AX = mybir.AxisListType.X

D = 1024
T = 2048
C8 = 8
NSEQ = 2
NCORE = 8
AR_ELEMS = 48640

ENGINES = ("tensor", "vector", "scalar", "gpsimd", "sync")

A_SW1, A_SW3, A_GB, A_SWK, A_SILU = 0, 16384, 32768, 40960, 43008
A_SQ, A_RB, A_MB = 43520, 47616, 48128
A_EW1, A_EW2, A_XR, A_HB = 0, 16384, 32768, 34816
A_KFM, A_VTM, A_WQ, A_ATT, A_ATS, A_QBF = 0, 16384, 32768, 40960, 45056, 47104


class Em:
    def __init__(self):
        self.ops = {e: [] for e in ENGINES}
        self.counts = {}
        self.sems = {}

    def sem(self, name):
        self.counts[name] = 0

    def wait(self, eng, toks):
        best = {}
        for tk in toks:
            if tk is None:
                continue
            n, v = tk
            if v > best.get(n, -1):
                best[n] = v
        for n, v in sorted(best.items()):
            if v > 0:
                self.ops[eng].append(lambda e, n=n, v=v: e.wait_ge(self.sems[n], v))

    def op(self, eng, builder, waits=(), inc=None, inc_n=1):
        self.wait(eng, waits)
        tok = None
        if inc is not None:
            self.counts[inc] += inc_n
            tok = (inc, self.counts[inc])
            self.ops[eng].append(
                lambda e, b=builder, s=inc, n=inc_n: b(e).then_inc(self.sems[s], n))
        else:
            self.ops[eng].append(lambda e, b=builder: b(e))
        return tok

    def tok(self, sem):
        return (sem, self.counts[sem])

    def barrier(self):
        self.counts["BAR"] += len(ENGINES)
        tgt = self.counts["BAR"]
        for eng in ENGINES:
            if eng == "vector":
                self.ops[eng].append(
                    lambda e: e.tensor_copy(self.bsc[:, 0:2], self.bsc[:, 2:4])
                    .then_inc(self.sems["BAR"], 1))
            elif eng == "scalar":
                self.ops[eng].append(
                    lambda e: e.copy(self.bsc[:, 4:6], self.bsc[:, 6:8])
                    .then_inc(self.sems["BAR"], 1))
            else:
                self.ops[eng].append(lambda e: e.nop().then_inc(self.sems["BAR"], 1))
            self.ops[eng].append(lambda e, v=tgt: e.wait_ge(self.sems["BAR"], v))

    def build(self, nc):
        with contextlib.ExitStack() as st:
            for name in self.counts:
                self.sems[name] = st.enter_context(nc.semaphore(name))
            block = st.enter_context(nc.Block())
            for eng in ENGINES:
                getattr(block, eng)(lambda e, eng=eng: [f(e) for f in self.ops[eng]])


class G:
    pass


def ar(g, off, n):
    return g.AR[:, off: off + n]




def declare_inputs(nc, parts):
    d = {}

    def inp(name, shape, dtype=F32):
        d[name] = nc.dram_tensor(name, list(shape), dtype, kind="ExternalInput").ap()

    inp("xT", (NSEQ, D, T))
    d["outT"] = nc.dram_tensor("outT", [NSEQ, D, T], F32, kind="ExternalOutput").ap()
    d["dbg"] = nc.dram_tensor("dbg", [128, 8192], F32, kind="ExternalOutput").ap()
    inp("ident", (128, 128))
    inp("ones_col", (1, 128))
    inp("ones_red", (128, 1))
    inp("ones_row", (128, 512))
    if "l0ffn" in parts:
        inp("sw1", (D, 2048)); inp("sw3", (D, 2048)); inp("sw2", (2048, D))
        inp("sb1fm", (128, 16)); inp("sb3fm", (128, 16))
        inp("rowpack", (128, 3072))
        inp("w1", (8, D, D)); inp("w2", (8, D, D))
        inp("b1fm", (128, 64))
        d["sw2c"] = nc.dram_tensor("sw2c", [16, 128, D], BF16, kind="Internal").ap()
    if "l0attn" in parts:
        inp("wq", (D, D)); inp("wk", (D, D)); inp("wv", (D, D))
    if "l1attn" in parts:
        inp("wqkv", (D, 3 * D)); inp("wo", (D, D))
        inp("hsel16", (128, 128)); inp("hselT", (16, 1024))
        d["woc"] = nc.dram_tensor("woc", [8, 128, D], BF16, kind="Internal").ap()
    if ("l1attn" in parts) or ("l1ffn" in parts):
        inp("triu128", (128, 128))
    if "l1ffn" in parts:
        inp("routerT", (D, 32)); inp("rbrow", (1, 32))
        inp("w1L1", (32, D, 256)); inp("w2L1", (32, 256, D))
        inp("b1L1r", (32, 256)); inp("b2L1r", (32, D))
        inp("eoffrow", (1, 32)); inp("ones128", (128, 128))
        d["xg_dram"] = nc.dram_tensor("xg_dram", [32 * 128 + 8, D], BF16, kind="Internal").ap()
        d["yg_dram"] = nc.dram_tensor("yg_dram", [32 * 128 + 8, D], BF16, kind="Internal").ap()
    return d


def alloc_state(g, parts):
    nc = g.nc
    g.X = nc.alloc_sbuf_tensor("X", [128, C8 * T], F32)
    g.XN = nc.alloc_sbuf_tensor("XN", [128, C8 * T], BF16)
    g.AR = nc.alloc_sbuf_tensor("AR", [128, AR_ELEMS], BF16)
    g.ident = nc.alloc_sbuf_tensor("identb", [128, 128], BF16)
    g.ones_col = nc.alloc_sbuf_tensor("ones_colb", [1, 128], BF16)
    g.ones_red = nc.alloc_sbuf_tensor("ones_redb", [128, 1], BF16)
    g.ones_row = nc.alloc_sbuf_tensor("ones_rowb", [128, 512], BF16)
    if "l0ffn" in parts:
        g.sb1fm = nc.alloc_sbuf_tensor("sb1fm_sb", [128, 16], F32)
        g.sb3fm = nc.alloc_sbuf_tensor("sb3fm_sb", [128, 16], F32)
        g.rowpack = nc.alloc_sbuf_tensor("rowpackb", [128, 3072], BF16)
        g.b1fm = nc.alloc_sbuf_tensor("b1fm_sb", [128, 64], F32)
    if "l0attn" in parts:
        g.sm1 = nc.alloc_sbuf_tensor("sm1", [128, 8], F32)
    if "l1attn" in parts:
        g.hsel16 = nc.alloc_sbuf_tensor("hsel16b", [128, 128], BF16)
        g.hselT = nc.alloc_sbuf_tensor("hselTb", [16, 1024], BF16)
    if ("l1attn" in parts) or ("l1ffn" in parts):
        g.triu128 = nc.alloc_sbuf_tensor("triu128b", [128, 128], BF16)
    if "l1ffn" in parts:
        g.rbrow = nc.alloc_sbuf_tensor("rbrowb", [1, 32], BF16)
        g.eoffrow = nc.alloc_sbuf_tensor("eoffrow_sb", [1, 32], BF16)
        g.ones128 = nc.alloc_sbuf_tensor("ones128b", [128, 128], BF16)
        g.routert = nc.alloc_sbuf_tensor("routert", [128, 8 * 32], BF16)
    g.em.bsc = nc.alloc_sbuf_tensor("bar_scratch", [128, 8], F32)
    g.ps_big = nc.alloc_psum_tensor("ps_big", [128, 2048], F32)
    g.ps_acc = nc.alloc_psum_tensor("ps_acc", [128, 1024], F32)
    g.ps_s1 = nc.alloc_psum_tensor("ps_s1", [128, 512], F32)
    g.ps_s2 = nc.alloc_psum_tensor("ps_s2", [128, 512], F32)


def emit_load_consts(g, parts):
    em = g.em
    dd = g.d

    def ld(dst, src):
        return em.op("gpsimd", lambda e, a=dst, b=src: e.dma_start(a, b),
                     inc="LD", inc_n=16)

    ld(g.ident[:], dd["ident"])
    ld(g.ones_col[:], dd["ones_col"])
    ld(g.ones_red[:], dd["ones_red"])
    tk = ld(g.ones_row[:], dd["ones_row"])
    if "l0ffn" in parts:
        ld(g.sb1fm[:], dd["sb1fm"]); ld(g.sb3fm[:], dd["sb3fm"])
        ld(g.rowpack[:], dd["rowpack"]); tk = ld(g.b1fm[:], dd["b1fm"])
    if "l1attn" in parts:
        ld(g.hsel16[:], dd["hsel16"]); tk = ld(g.hselT[:], dd["hselT"])
    if ("l1attn" in parts) or ("l1ffn" in parts):
        tk = ld(g.triu128[:], dd["triu128"])
    if "l1ffn" in parts:
        ld(g.rbrow[:], dd["rbrow"]); ld(g.eoffrow[:], dd["eoffrow"])
        ld(g.ones128[:], dd["ones128"])
        tk = ld(g.routert[:].rearrange("p (c e) -> p c e", c=8), dd["routerT"].rearrange("(c p) e -> p c e", p=128))
    return tk




def emit_ln(g, pre):
    em = g.em
    mm = g.mm
    X, XN = g.X, g.XN
    inv = 1.0 / D
    rrow = lambda c: g.AR[0:1, A_SQ + (c.start if isinstance(c, slice) else 0):
                          A_SQ + (c.stop if isinstance(c, slice) else T)]
    mrow = lambda c: g.AR[0:1, A_SQ + T + (c.start if isinstance(c, slice) else 0):
                          A_SQ + T + (c.stop if isinstance(c, slice) else T)]

    cast = []
    for ci in range(C8):
        tk = em.op("vector",
                   lambda e, a=XN[:, ci * T:(ci + 1) * T], b=X[:, ci * T:(ci + 1) * T]:
                   e.tensor_copy(a, b),
                   waits=[pre] if ci == 0 else (), inc="DVE")
        cast.append(tk)
    for ci in range(C8):
        for nt in range(4):
            tk_pe = mm(g.ps_big[0:1, nt * 512: nt * 512 + 512], g.ones_red[:],
                       XN[:, ci * T + nt * 512: ci * T + nt * 512 + 512],
                       start=(ci == 0), stop=(ci == C8 - 1),
                       waits=[cast[ci], g.ln_ps_free] if nt == 0 else (),
                       inc="PE" if (ci == C8 - 1 and nt == 3) else None)
    t1 = em.op("vector", lambda e, a=g.ps_big[0:1, 0:2048]: e.tensor_scalar_mul(a, a, inv),
               waits=[tk_pe], inc="DVE")
    t2 = em.op("scalar", lambda e, a=g.AR[0:1, A_SQ + T: A_SQ + 2 * T],
               b=g.ps_big[0:1, 0:2048]: e.copy(a, b), waits=[t1], inc="ACT")

    sub_done = []
    for nt in range(4):
        bc = mm(g.ps_s1[:, :], g.ones_col[:], mrow(slice(nt * 512, nt * 512 + 512)),
                start=True, stop=True,
                waits=[t2] + ([em.tok("DVE")] if nt else []), inc="PE")
        ev = em.op("scalar", lambda e, a=ar(g, A_MB, 512), b=g.ps_s1[:, :]:
                   e.copy(a, b), waits=[bc], inc="ACT")
        for ci in range(C8):
            cc = XN[:, ci * T + nt * 512: ci * T + nt * 512 + 512]
            tk = em.op("vector", lambda e, a=cc, b=ar(g, A_MB, 512):
                       e.tensor_sub(a, a, b),
                       waits=[ev] if ci == 0 else (), inc="DVE")
        sub_done.append(tk)

    sq_pe = {}
    for ci in range(C8):
        w = [sub_done[-1]]
        if ci >= 2:
            w.append(sq_pe[ci - 2])
        tk_sq = em.op("vector",
                      lambda e, a=ar(g, A_SQ + (ci % 2) * T, T), b=XN[:, ci * T:(ci + 1) * T]:
                      e.tensor_mul(a, b, b), waits=w, inc="DVE")
        for nt in range(4):
            sq_pe[ci] = mm(g.ps_big[0:1, nt * 512: nt * 512 + 512], g.ones_red[:],
                           ar(g, A_SQ + (ci % 2) * T + nt * 512, 512),
                           start=(ci == 0), stop=(ci == C8 - 1),
                           waits=[tk_sq, t2] if nt == 0 else (), inc="PE")
    v1 = em.op("vector", lambda e, a=g.ps_big[0:1, 0:2048]:
               e.tensor_scalar(a, a, inv, 1e-5, ALU.mult, ALU.add),
               waits=[sq_pe[C8 - 1]], inc="DVE")
    v2 = em.op("vector", lambda e, a=g.ps_big[0:1, 0:2048]: e.reciprocal(a, a),
               waits=[v1], inc="DVE")
    v3 = em.op("scalar", lambda e, a=g.AR[0:1, A_SQ: A_SQ + T],
               b=g.ps_big[0:1, 0:2048]: e.activation(a, b, AF.Sqrt),
               waits=[v2], inc="ACT")

    for nt in range(4):
        bc = mm(g.ps_s1[:, :], g.ones_col[:], rrow(slice(nt * 512, nt * 512 + 512)),
                start=True, stop=True, waits=[v3, em.tok("DVE")], inc="PE")
        ev = em.op("scalar", lambda e, a=ar(g, A_RB, 512), b=g.ps_s1[:, :]:
                   e.copy(a, b), waits=[bc], inc="ACT")
        for ci in range(C8):
            cc = XN[:, ci * T + nt * 512: ci * T + nt * 512 + 512]
            tk = em.op("vector", lambda e, a=cc, b=ar(g, A_RB, 512):
                       e.tensor_mul(a, a, b),
                       waits=[ev] if ci == 0 else (), inc="DVE")
    g.ln_ps_free = em.tok("DVE")
    return em.tok("DVE")




def emit_l0ffn(g, xn_tok):
    em = g.em
    mm = g.mm
    dd = g.d
    X, XN = g.X, g.XN

    t_sw1 = em.op("gpsimd", lambda e: e.dma_start(
        ar(g, A_SW1, 16384).rearrange("p (c h) -> p c h", c=8), dd["sw1"].rearrange("(c p) h -> p c h", p=128)),
        waits=[g.arena_free], inc="LD", inc_n=16)
    t_sw3 = em.op("gpsimd", lambda e: e.dma_start(
        ar(g, A_SW3, 16384).rearrange("p (c h) -> p c h", c=8), dd["sw3"].rearrange("(c p) h -> p c h", p=128)),
        inc="LD", inc_n=16)
    if SW2DIRECT:
        g.sw2_cached = True
        sw2_ready = [None] * 16
    elif not g.sw2_cached:
        g.sw2_cached = True
        st_toks = []
        for k in range(16):
            slot = ar(g, A_SWK + (k % 2) * D, D)
            w = [g.arena_free] + ([st_toks[k - 2]] if k >= 2 else [])
            ck = em.op("gpsimd", lambda e, a=slot, k=k: e.dma_start(
                a, dd["sw2"][k * 128:(k + 1) * 128, :]), waits=w, inc="LD", inc_n=16)
            st = em.op("sync", lambda e, a=slot, k=k: e.dma_start(dd["sw2c"][k], a),
                       waits=[ck], inc="IO", inc_n=16)
            st_toks.append(st)
        sw2_ready = [st_toks[-1]] * 16
    else:
        sw2_ready = [None] * 16

    prev_out_evac = None
    prev_swk = {}
    for tt in range(4):
        tw = slice(tt * 512, tt * 512 + 512)
        g_first = True
        for k in range(16):
            for ci in range(C8):
                w = []
                if g_first and ci == 0:
                    w = [t_sw1, t_sw3, xn_tok] + ([prev_out_evac] if prev_out_evac else [])
                mm(g.ps_s1[:, :], ar(g, A_SW1 + ci * 2048 + k * 128, 128),
                   XN[:, ci * T + tt * 512: ci * T + tt * 512 + 512],
                   start=(ci == 0), stop=(ci == C8 - 1), waits=w)
            for ci in range(C8):
                tk_pe = mm(g.ps_s2[:, :], ar(g, A_SW3 + ci * 2048 + k * 128, 128),
                           XN[:, ci * T + tt * 512: ci * T + tt * 512 + 512],
                           start=(ci == 0), stop=(ci == C8 - 1),
                           inc="PE" if ci == C8 - 1 else None)
            a = em.op("scalar", lambda e, k=k: e.activation(
                ar(g, A_SILU, 512), g.ps_s1[:, :], AF.Silu, bias=g.sb1fm[:, k:k + 1]),
                waits=[tk_pe], inc="ACT")
            v = em.op("vector", lambda e, k=k: e.scalar_tensor_tensor(
                ar(g, A_GB + k * 512, 512), g.ps_s2[:, :], g.sb3fm[:, k:k + 1],
                ar(g, A_SILU, 512), ALU.add, ALU.mult),
                waits=[a], inc="DVE")
            em.wait("tensor", [a, v])
            g_first = False
        g_done = em.tok("DVE")
        if tt == 3:
            g.sw13_free = em.tok("PE")

        outslot = [g.ps_big[:, j * 512:(j + 1) * 512] for j in range(4)] + \
                  [g.ps_acc[:, 0:512], g.ps_acc[:, 512:1024], g.ps_s1[:, :], g.ps_s2[:, :]]
        for co in range(C8):
            mm(outslot[co], g.rowpack[64:65, 2048 + co * 128: 2048 + co * 128 + 128],
               g.ones_row[64:65, :],
               start=True, stop=False, waits=[g_done] if co == 0 else ())
        for k in range(16):
            slot = ar(g, A_SWK + (k % 2) * D, D)
            if SW2DIRECT:
                lt = em.op("gpsimd", lambda e, a=slot, k=k: e.dma_start(
                    a, dd["sw2"][k * 128:(k + 1) * 128, :]),
                    waits=[sw2_ready[k], prev_swk.get(k % 2)], inc="LD", inc_n=16)
            else:
                lt = em.op("sync", lambda e, a=slot, k=k: e.dma_start(a, dd["sw2c"][k]),
                           waits=[sw2_ready[k], prev_swk.get(k % 2)], inc="IO", inc_n=16)
            for co in range(C8):
                tk_pe = mm(outslot[co], slot[:, co * 128: co * 128 + 128],
                           ar(g, A_GB + k * 512, 512),
                           start=False, stop=(k == 15),
                           waits=[lt] if co == 0 else (),
                           inc="PE" if co == C8 - 1 else None)
            prev_swk[k % 2] = tk_pe
        for co in [6, 7, 0, 1, 2, 3, 4, 5]:
            prev_out_evac = em.op("vector", lambda e, co=co, tt=tt: e.scalar_tensor_tensor(
                X[:, co * T + tt * 512: co * T + tt * 512 + 512], outslot[co], 0.25,
                X[:, co * T + tt * 512: co * T + tt * 512 + 512], ALU.mult, ALU.add),
                waits=[em.tok("PE")] if co == 6 else (), inc="DVE")

    shared_done = prev_out_evac

    order = [0, 5, 2, 7, 4, 1, 6, 3]
    pair = {r: (3 * r % 8, (3 * r + 7) % 8) for r in order}
    slot_of = {}

    def load_w1(e, s, waits=()):
        return em.op("gpsimd", lambda e_, x=e, s=s: e_.dma_start(
            ar(g, A_EW1 + s * 8192, 8192).rearrange("p (k h) -> p k h", k=8),
            dd["w1"][x].rearrange("(k p) h -> p k h", p=128)),
            waits=waits, inc="LD", inc_n=16)

    def load_w2(e, s, waits=()):
        return em.op("gpsimd", lambda e_, x=e, s=s: e_.dma_start(
            ar(g, A_EW2 + s * 8192, 8192).rearrange("p (k c) -> p k c", k=8),
            dd["w2"][x].rearrange("(k p) c -> p k c", p=128)),
            waits=waits, inc="LD", inc_n=16)

    e0, e1 = pair[order[0]]
    w = [shared_done, em.tok("PE")]
    slot_of[e0] = [0, load_w1(e0, 0, w), None]
    slot_of[e0][2] = load_w2(e0, 0)
    slot_of[e1] = [1, load_w1(e1, 1), None]
    slot_of[e1][2] = load_w2(e1, 1)

    moe_evac = None
    slot_act = {0: None, 1: None}
    for idx, r in enumerate(order):
        e0, e1 = pair[r]
        if idx + 1 < len(order):
            need = set(pair[order[idx + 1]])
            drop = e0 if e0 not in need else e1
            keep = e1 if drop == e0 else e0
            newe = next(iter(need - {keep}))
        else:
            drop, keep, newe = e0, e1, None
        gt = None
        for ci in range(C8):
            gt = em.op("vector", lambda e, ci=ci, r=r: e.tensor_copy(
                ar(g, A_XR + ci * 256, 256),
                XN[:, ci * T + r:(ci + 1) * T: 8]),
                waits=([xn_tok] + ([moe_evac] if moe_evac else [shared_done])) if ci == 0 else (),
                inc="DVE")
        h_pe_done = {}
        for eidx, e in enumerate((drop, keep)):
            s, tw1, tw2 = slot_of[e]
            for mh in range(8):
                pb = g.ps_acc[:, (mh % 2) * 512:(mh % 2) * 512 + 256]
                for ci in range(C8):
                    w = [tw1, gt, slot_act[mh % 2]] if ci == 0 else ()
                    tk_pe = mm(pb, ar(g, A_EW1 + s * 8192 + ci * 1024 + mh * 128, 128),
                               ar(g, A_XR + ci * 256, 256),
                               start=(ci == 0), stop=(ci == C8 - 1),
                               waits=w, inc="PE" if ci == C8 - 1 else None)
                slot_act[mh % 2] = em.op(
                    "scalar", lambda e_, x=e, mh=mh, ei=eidx, pb=pb: e_.activation(
                        ar(g, A_HB + ei * 2048 + mh * 256, 256), pb, AF.Gelu,
                        bias=g.b1fm[:, x * 8 + mh: x * 8 + mh + 1]),
                    waits=[tk_pe], inc="ACT")
            h_pe_done[e] = tk_pe
            if eidx == 0 and newe is not None:
                nw1 = load_w1(newe, slot_of[drop][0], waits=[tk_pe])
        h_done = em.tok("ACT")
        yslot = [g.ps_big[:, co * 256: co * 256 + 256] for co in range(C8)]
        for co in range(C8):
            for eidx, e in enumerate((drop, keep)):
                s, tw1, tw2 = slot_of[e]
                rp, rs = 32 * (e % 3), e // 3
                mm(yslot[co], g.rowpack[rp:rp + 1, rs * 1024 + co * 128: rs * 1024 + co * 128 + 128],
                   g.ones_row[rp:rp + 1, 0:256],
                   start=(eidx == 0), stop=False,
                   waits=([h_done, tw2] + ([moe_evac] if moe_evac else []))
                   if co == 0 else ())
                for k in range(8):
                    tk_pe = mm(yslot[co], ar(g, A_EW2 + s * 8192 + k * 1024 + co * 128, 128),
                               ar(g, A_HB + eidx * 2048 + k * 256, 256),
                               start=False, stop=(eidx == 1 and k == 7),
                               inc="PE" if (eidx == 1 and k == 7) else None)
        y_done = tk_pe
        if newe is not None:
            nw2 = load_w2(newe, slot_of[drop][0], waits=[y_done])
            slot_of[newe] = [slot_of[drop][0], nw1, nw2]
            del slot_of[drop]
        for co in range(C8):
            moe_evac = em.op("vector", lambda e, co=co, r=r: e.scalar_tensor_tensor(
                X[:, co * T + r:(co + 1) * T: 8], g.ps_big[:, co * 256: co * 256 + 256],
                0.125, X[:, co * T + r:(co + 1) * T: 8], ALU.mult, ALU.add),
                waits=[y_done] if co == 0 else (), inc="DVE")
    g.arena_free = moe_evac
    g.ln_ps_free = moe_evac
    return moe_evac




def emit_l0attn(g, xn_tok):
    em = g.em
    mm = g.mm
    dd = g.d
    X, XN = g.X, g.XN
    WB = lambda off, n: ar(g, A_WQ + off, n)
    psb = g.ps_acc[:].bitcast(BF16)

    def load_w(name, waits):
        return em.op("gpsimd", lambda e, nm=name: e.dma_start(
            WB(0, 8192).rearrange("p (k d) -> p k d", k=8),
            dd[nm].rearrange("(k p) d -> p k d", p=128)),
            waits=waits, inc="LD", inc_n=16)

    t_wk = load_w("wk", [g.arena_free])
    pslots = [g.ps_s1, g.ps_s2]
    ev_prev = {0: None, 1: None}
    for i, (dc, nt) in enumerate([(d, n) for d in range(C8) for n in range(4)]):
        pb = pslots[i % 2][:, :]
        for ci in range(C8):
            w = []
            if ci == 0:
                w = [t_wk if i == 0 else None, xn_tok if i == 0 else None,
                     ev_prev[i % 2]]
            tk_pe = mm(pb, WB(ci * 1024 + dc * 128, 128),
                       XN[:, ci * T + nt * 512: ci * T + nt * 512 + 512],
                       start=(ci == 0), stop=(ci == C8 - 1),
                       waits=w, inc="PE" if ci == C8 - 1 else None)
        ev_prev[i % 2] = em.op("scalar", lambda e, a=ar(g, A_KFM + dc * 2048 + nt * 512, 512), b=pb:
                               e.copy(a, b), waits=[tk_pe], inc="ACT")
    kproj_pe = em.tok("PE")

    t_wv = load_w("wv", [kproj_pe])
    for i, (tc, dh) in enumerate([(t, d) for t in range(16) for d in range(2)]):
        pb = pslots[i % 2][:, :]
        for ci in range(C8):
            w = []
            if ci == 0:
                w = [t_wv if i == 0 else None, ev_prev[i % 2]]
            tk_pe = mm(pb, XN[:, ci * T + tc * 128: ci * T + tc * 128 + 128],
                       WB(ci * 1024 + dh * 512, 512),
                       start=(ci == 0), stop=(ci == C8 - 1),
                       waits=w, inc="PE" if ci == C8 - 1 else None)
        ev_prev[i % 2] = em.op("scalar", lambda e, a=ar(g, A_VTM + tc * 1024 + dh * 512, 512), b=pb:
                               e.copy(a, b), waits=[tk_pe], inc="ACT")
    vproj_pe = em.tok("PE")
    kv_ev = em.tok("ACT")

    t_wq = load_w("wq", [vproj_pe])
    prev_grp_evac = None
    prev_tr_evacs = [kv_ev]
    prev_scores_pe = None
    prev_tr_pe = None
    exp_toks = []
    for grp in range(8):
        tr_evacs_grp = []
        for qs in range(2):
            qt = grp * 2 + qs
            for dc in range(C8):
                for ci in range(C8):
                    w = []
                    if dc == 0 and ci == 0:
                        w = [t_wq if qt == 0 else None] + prev_tr_evacs
                    tk_pe = mm(g.ps_acc[:, dc * 128: dc * 128 + 128],
                               WB(ci * 1024 + dc * 128, 128),
                               XN[:, ci * T + qt * 128: ci * T + qt * 128 + 128],
                               start=(ci == 0), stop=(ci == C8 - 1),
                               waits=w, inc="PE" if (dc == C8 - 1 and ci == C8 - 1) else None)
            qev = em.op("scalar", lambda e, a=ar(g, A_QBF, 1024), b=g.ps_acc[:, :]:
                        e.copy(a, b), waits=[tk_pe, prev_scores_pe], inc="ACT")
            for nk in range(4):
                for dc in range(C8):
                    w = []
                    if nk == 0 and dc == 0:
                        w = [qev, prev_grp_evac]
                    tk_pe = mm(g.ps_big[:, nk * 512: nk * 512 + 512],
                               ar(g, A_QBF + dc * 128, 128),
                               ar(g, A_KFM + dc * 2048 + nk * 512, 512),
                               start=(dc == 0), stop=(dc == C8 - 1),
                               waits=w, inc="PE" if (nk == 3 and dc == C8 - 1) else None)
            prev_scores_pe = tk_pe
            m1 = em.op("vector", lambda e: e.reduce_max(g.sm1[:, 0:1], g.ps_big[:, 0:2048], axis=AX),
                       waits=[tk_pe], inc="DVE")
            m2 = em.op("vector", lambda e: e.tensor_scalar_mul(g.sm1[:, 1:2], g.sm1[:, 0:1], -1.0 / 32.0),
                       waits=[m1], inc="DVE")
            em.wait("vector", [m2])
            ex = em.op("scalar", lambda e: e.activation(
                ar(g, A_ATS, 2048), g.ps_big[:, 0:2048], AF.Exp,
                bias=g.sm1[:, 1:2], scale=1.0 / 32.0, accum_out=g.sm1[:, 3:4]),
                waits=[m2, prev_tr_pe], inc="ACT")
            exp_toks.append(ex)
            rc = em.op("vector", lambda e: e.reciprocal(g.sm1[:, 2:3], g.sm1[:, 3:4]),
                       waits=[ex], inc="DVE")
            sc = em.op("vector", lambda e: e.tensor_scalar_mul(
                ar(g, A_ATS, 2048), ar(g, A_ATS, 2048), g.sm1[:, 2:3]),
                waits=[rc], inc="DVE")
            tr_evacs = []
            for j16 in range(16):
                tk_pe = em.op("tensor", lambda t, o=psb[:, j16 * 128: j16 * 128 + 128],
                              i_=ar(g, A_ATS + j16 * 128, 128):
                              t.transpose(o, i_, g.ident[:]),
                              waits=([sc] + prev_tr_evacs) if j16 == 0 else (),
                              inc="PE" if j16 == 15 else None)
            for half in range(2):
                dst = g.AR[:, A_ATT + half * 8 * 256 + qs * 128:
                           A_ATT + half * 8 * 256 + qs * 128 + 8 * 256] \
                    .rearrange("p (k q) -> p k q", k=8)[:, :, 0:128]
                src = psb[:, half * 1024:(half + 1) * 1024].rearrange("p (k q) -> p k q", k=8)
                if half == 0:
                    tr_evacs.append(em.op("scalar", lambda e, a=dst, bb=src: e.copy(a, bb),
                                          waits=[tk_pe], inc="ACT"))
                else:
                    tr_evacs.append(em.op("vector", lambda e, a=dst, bb=src: e.tensor_copy(a, bb),
                                          waits=[tk_pe], inc="DVE"))
            prev_tr_pe = em.tok("PE")
            prev_tr_evacs = tr_evacs
            tr_evacs_grp += tr_evacs
        for dc in range(C8):
            for kc in range(16):
                w = []
                if kc == 0 and dc == 0:
                    w = tr_evacs_grp + [exp_toks[-1]]
                tk_pe = mm(g.ps_big[:, dc * 256: dc * 256 + 256],
                           ar(g, A_VTM + kc * 1024 + dc * 128, 128),
                           ar(g, A_ATT + kc * 256, 256),
                           start=(kc == 0), stop=(kc == 15),
                           waits=w, inc="PE" if (kc == 15 and dc == C8 - 1) else None)
        for dc in range(C8):
            prev_grp_evac = em.op("vector", lambda e, dc=dc, grp=grp: e.scalar_tensor_tensor(
                X[:, dc * T + grp * 256: dc * T + grp * 256 + 256],
                g.ps_big[:, dc * 256: dc * 256 + 256], 1.0,
                X[:, dc * T + grp * 256: dc * T + grp * 256 + 256], ALU.mult, ALU.add),
                waits=[tk_pe] if dc == 0 else (), inc="DVE")
    g.arena_free = prev_grp_evac
    g.ln_ps_free = prev_grp_evac
    return prev_grp_evac



A1_W, A1_Q, A1_K, A1_V, A1_Y = 0, 24576, 28672, 32768, 36864
A1_KT, A1_KLN, A1_SF, A1_SB, A1_KSQ, A1_NM, A1_MSB, A1_SCT, A1_WO = (
    40960, 41984, 43008, 44032, 44544, 45056, 45568, 46080, 46592)


def emit_l1attn(g, xn_tok):
    em = g.em
    mm = g.mm
    dd = g.d
    X, XN = g.X, g.XN
    psb2 = g.ps_acc[:].bitcast(BF16)
    sf32 = g.AR[:, A1_SF: A1_SF + 1024].bitcast(F32)

    t_w = em.op("gpsimd", lambda e: e.dma_start(
        ar(g, A1_W, 24576).rearrange("p (c o) -> p c o", c=8),
        dd["wqkv"].rearrange("(c p) o -> p c o", p=128)),
        waits=[g.arena_free], inc="LD", inc_n=16)
    wo_ready = [None] * 8

    em.op("vector", lambda e: e.memset(sf32[:, :], 0.0), waits=[g.arena_free], inc="DVE")
    sbf_tok = em.op("vector", lambda e: e.memset(ar(g, A1_SB, 512), 0.0), inc="DVE")

    prev_wo_evac = None
    prev_swo = {}
    kt_free = None
    sup_free = None
    for tile in range(4):
        tw = tile * 512

        def proj_wave(dst_off, builder_lhs, waits0):
            evs = []
            for wv in range(2):
                for m4 in range(4):
                    for ci in range(C8):
                        w = []
                        if m4 == 0 and ci == 0:
                            w = list(waits0) if wv == 0 else [evs[-1]]
                        lhs, rhs = builder_lhs(wv * 4 + m4, ci)
                        tk_pe = mm(g.ps_big[:, m4 * 512: m4 * 512 + 512], lhs, rhs,
                                   start=(ci == 0), stop=(ci == C8 - 1),
                                   waits=w, inc="PE" if (m4 == 3 and ci == C8 - 1) else None)
                evs.append(em.op("vector", lambda e, a=ar(g, dst_off + wv * 2048, 2048),
                                 b=g.ps_big[:, 0:2048]: e.tensor_copy(a, b),
                                 waits=[tk_pe], inc="DVE"))
            return evs[-1]

        def q_lhs(m, ci, osec=0):
            return (ar(g, A1_W + ci * 3072 + osec * 1024 + m * 128, 128),
                    XN[:, ci * T + tw: ci * T + tw + 512])

        def v_lhs(m, ci):
            tc, dh = m // 2, m % 2
            return (XN[:, ci * T + tw + tc * 128: ci * T + tw + tc * 128 + 128],
                    ar(g, A1_W + ci * 3072 + 2048 + dh * 512, 512))

        w0 = [t_w, xn_tok, prev_wo_evac, sbf_tok]
        q_ev = proj_wave(A1_Q, lambda m, ci: q_lhs(m, ci, 0), w0)
        k_ev = proj_wave(A1_K, lambda m, ci: q_lhs(m, ci, 1), [q_ev])
        v_ev = proj_wave(A1_V, v_lhs, [k_ev])

        if DBG_L1A and tile == 0 and not g.dbg_done2:
            em.op("gpsimd", lambda e, a=g.AR[:, A1_K: A1_K + 2048]:
                  e.dma_start(g.d["dbg"][:, 4096:6144], a), waits=[k_ev], inc="LD", inc_n=16)
            em.op("gpsimd", lambda e, a=g.hsel16[:]:
                  e.dma_start(g.d["dbg"][:, 6144:6272], a), waits=[k_ev], inc="LD", inc_n=16)
            g.dbg_done2 = True
        sq_ev = {}
        for cc in range(C8):
            tkq = em.op("vector", lambda e, a=ar(g, A1_KSQ + (cc % 2) * 512, 512),
                        b=ar(g, A1_K + cc * 512, 512): e.tensor_mul(a, b, b),
                        waits=[k_ev] + ([sq_ev[cc - 2]] if cc >= 2 else []), inc="DVE")
            mm(g.ps_s1[0:16, 0:512], g.hsel16[:, cc * 16: cc * 16 + 16],
               ar(g, A1_K + cc * 512, 512), start=(cc == 0), stop=(cc == C8 - 1),
               waits=[k_ev, sup_free] if cc == 0 else ())
            sq_ev[cc] = mm(g.ps_s2[0:16, 0:512], g.hsel16[:, cc * 16: cc * 16 + 16],
                           ar(g, A1_KSQ + (cc % 2) * 512, 512),
                           start=(cc == 0), stop=(cc == C8 - 1),
                           waits=[tkq, sup_free] if cc == 0 else [tkq], inc="PE")
        st_pe = em.tok("PE")
        inv64 = 1.0 / 64.0
        em.op("vector", lambda e: e.tensor_scalar_mul(g.ps_s1[0:16, 0:512], g.ps_s1[0:16, 0:512], inv64),
              waits=[st_pe], inc="DVE")
        em.op("vector", lambda e: e.tensor_scalar_mul(g.ps_s2[0:16, 0:512], g.ps_s2[0:16, 0:512], inv64),
              waits=[em.tok("DVE")], inc="DVE")
        em.op("scalar", lambda e: e.copy(g.AR[0:16, A1_MSB: A1_MSB + 512], g.ps_s1[0:16, 0:512]),
              waits=[em.tok("DVE")], inc="ACT")
        em.op("vector", lambda e: e.scalar_tensor_tensor(
            g.AR[0:16, A1_NM: A1_NM + 512], g.AR[0:16, A1_MSB: A1_MSB + 512], -1.0,
            g.AR[0:16, A1_MSB: A1_MSB + 512], ALU.mult, ALU.mult),
            waits=[em.tok("ACT")], inc="DVE")
        em.op("vector", lambda e: e.tensor_add(g.ps_s2[0:16, 0:512], g.ps_s2[0:16, 0:512],
                                               g.AR[0:16, A1_NM: A1_NM + 512]),
              waits=[em.tok("DVE")], inc="DVE")
        em.op("vector", lambda e: e.tensor_scalar_add(g.ps_s2[0:16, 0:512], g.ps_s2[0:16, 0:512], 1e-5),
              waits=[em.tok("DVE")], inc="DVE")
        s6 = em.op("vector", lambda e: e.reciprocal(g.ps_s2[0:16, 0:512], g.ps_s2[0:16, 0:512]),
                   waits=[em.tok("DVE")], inc="DVE")
        rsb = em.op("scalar", lambda e: e.activation(g.AR[0:16, A1_KLN: A1_KLN + 512],
                                                     g.ps_s2[0:16, 0:512], AF.Sqrt),
                    waits=[s6], inc="ACT")
        mrs = em.op("vector", lambda e: e.tensor_mul(g.AR[0:16, A1_KLN + 512: A1_KLN + 1024],
                                                     g.ps_s1[0:16, 0:512],
                                                     g.AR[0:16, A1_KLN: A1_KLN + 512]),
                    waits=[rsb], inc="DVE")
        for half in range(2):
            for cc4 in range(4):
                cc = half * 4 + cc4
                bcast_pe = mm(g.ps_big[:, cc4 * 512: cc4 * 512 + 512],
                              g.hselT[:, cc * 128: cc * 128 + 128],
                              g.AR[0:16, A1_KLN: A1_KLN + 512],
                              start=True, stop=True,
                              waits=[mrs, em.tok("DVE")] if cc4 == 0 else (),
                              inc="PE" if cc4 == 3 else None)
            t_mul = em.op("vector", lambda e, a=ar(g, A1_K + half * 2048, 2048),
                          b=g.ps_big[:, 0:2048]: e.tensor_mul(a, a, b),
                          waits=[bcast_pe], inc="DVE")
            for cc4 in range(4):
                cc = half * 4 + cc4
                bcast_pe = mm(g.ps_big[:, cc4 * 512: cc4 * 512 + 512],
                              g.hselT[:, cc * 128: cc * 128 + 128],
                              g.AR[0:16, A1_KLN + 512: A1_KLN + 1024],
                              start=True, stop=True,
                              waits=[t_mul] if cc4 == 0 else (),
                              inc="PE" if cc4 == 3 else None)
            em.op("vector", lambda e, a=ar(g, A1_K + half * 2048, 2048),
                  b=g.ps_big[:, 0:2048]: e.tensor_sub(a, a, b),
                  waits=[bcast_pe], inc="DVE")
        kn_ev = em.tok("DVE")
        if DBG_L1A and tile == 0 and not g.dbg_done:
            em.op("gpsimd", lambda e, a=g.AR[:, A1_K: A1_K + 2048]:
                  e.dma_start(g.d["dbg"][:, 0:2048], a), waits=[kn_ev], inc="LD", inc_n=16)
            em.op("gpsimd", lambda e, a=g.AR[0:16, A1_KLN: A1_KLN + 1024]:
                  e.dma_start(g.d["dbg"][0:16, 2048:3072], a), waits=[kn_ev], inc="LD", inc_n=16)
            em.op("gpsimd", lambda e, a=g.AR[0:16, A1_MSB: A1_MSB + 1024]:
                  e.dma_start(g.d["dbg"][0:16, 3072:4096], a), waits=[kn_ev], inc="LD", inc_n=16)
            g.dbg_done = True

        prev_mask = {0: None, 1: None}
        wave_evac = {0: None, 1: prev_wo_evac}
        for tc in range(4):
            for j in range(C8):
                tk_pe = em.op("tensor", lambda t, o=psb2[:, j * 128: j * 128 + 128],
                              i_=g.AR[:, A1_K + j * 512 + tc * 128: A1_K + j * 512 + tc * 128 + 128]:
                              t.transpose(o, i_, g.ident[:]),
                              waits=[kn_ev, v_ev, kt_free] if j == 0 else (),
                              inc="PE" if j == C8 - 1 else None)
            kt_ev = em.op("vector", lambda e, a=ar(g, A1_KT, 1024), b=psb2[:, 0:1024]:
                          e.tensor_copy(a, b), waits=[tk_pe], inc="DVE")
            y_wave_pe = {}
            for h in range(16):
                slot = h % 2
                sc_ps = g.ps_big[:, slot * 512: slot * 512 + 128]
                rbase = 64 * (h % 2)
                cc = h // 2
                kq_off = cc * 512 + tc * 128
                tk_sc = mm(sc_ps,
                           g.AR[rbase:rbase + 64, A1_K + kq_off: A1_K + kq_off + 128],
                           g.AR[rbase:rbase + 64, A1_Q + kq_off: A1_Q + kq_off + 128],
                           start=True, stop=True,
                           waits=[prev_mask[slot]], inc="PE")
                msk = em.op("vector", lambda e,
                            a=g.AR[:, A1_SCT + slot * 128: A1_SCT + slot * 128 + 128],
                            b=sc_ps, c=g.triu128[:]: e.tensor_tensor(a, b, c, ALU.mult),
                            waits=[tk_sc], inc="DVE")
                prev_mask[slot] = msk
                if DBG_L1A and tile == 0 and tc == 0 and h in (0, 2) and not g.dbg_sct.get(h):
                    msk = em.op("gpsimd", lambda e, a=g.AR[:, A1_SCT + slot * 128: A1_SCT + slot * 128 + 128], h=h:
                                e.dma_start(g.d["dbg"][:, 2048 + (h // 2) * 128: 2048 + (h // 2) * 128 + 128], a),
                                waits=[msk], inc="LD", inc_n=16)
                    prev_mask[slot] = msk
                    g.dbg_sct[h] = True
                pair = (h // 2) % 4
                wv = h // 8
                yps = g.ps_s1[rbase:rbase + 64, pair * 128: pair * 128 + 128]
                w_in = [msk]
                if h % 8 == 0:
                    w_in.append(wave_evac[1] if h == 0 else wave_evac[0])
                mm(yps, g.AR[:, A1_V + tc * 1024 + h * 64: A1_V + tc * 1024 + h * 64 + 64],
                   g.AR[:, A1_SCT + slot * 128: A1_SCT + slot * 128 + 128],
                   start=True, stop=False, waits=w_in)
                tk_y = mm(yps, g.AR[rbase:rbase + 64, A1_SB + cc * 64: A1_SB + cc * 64 + 64],
                          g.AR[rbase:rbase + 64, A1_Q + kq_off: A1_Q + kq_off + 128],
                          start=False, stop=True,
                          inc="PE" if h % 8 == 7 else None)
                if h % 8 == 7:
                    y_wave_pe[wv] = tk_y
                    dst = g.AR[:, A1_Y + wv * 2048: A1_Y + wv * 2048 + 2048] \
                        .rearrange("p (dc q) -> p dc q", dc=4)[:, :, tc * 128: tc * 128 + 128]
                    src = g.ps_s1[:, 0:512].rearrange("p (dc q) -> p dc q", dc=4)
                    wave_evac[wv] = em.op("scalar", lambda e, a=dst, b=src: e.copy(a, b),
                                          waits=[tk_y], inc="ACT")
            for h in range(16):
                rbase = 64 * (h % 2)
                tk_up = mm(g.ps_s2[rbase:rbase + 64, (h // 2) * 64: (h // 2) * 64 + 64],
                           g.AR[:, A1_KT + h * 64: A1_KT + h * 64 + 64],
                           g.AR[:, A1_V + tc * 1024 + h * 64: A1_V + tc * 1024 + h * 64 + 64],
                           start=True, stop=True,
                           waits=[sup_free, kt_ev] if h == 0 else (),
                           inc="PE" if h == 15 else None)
            sadd = em.op("vector", lambda e: e.tensor_add(sf32[:, :], sf32[:, :], g.ps_s2[:, :]),
                         waits=[tk_up], inc="DVE")
            sup_free = sadd
            sbf_tok = em.op("vector", lambda e, a=ar(g, A1_SB, 512): e.tensor_copy(a, sf32[:, :]),
                            waits=[sadd], inc="DVE")
            kt_free = sbf_tok
            em.wait("tensor", [sbf_tok])
        y_done = wave_evac[1]
        if DBG_L1A and tile == 0 and not g.dbg_done3:
            em.op("gpsimd", lambda e, a=g.AR[:, A1_Y: A1_Y + 1920]:
                  e.dma_start(g.d["dbg"][:, 6272:8192], a),
                  waits=[wave_evac[0], wave_evac[1]], inc="LD", inc_n=16)
            g.dbg_done3 = True
        for wv in range(2):
            for k in range(C8):
                slot = ar(g, A1_WO + (k % 2) * D, D)
                lt = em.op("gpsimd", lambda e, a=slot, k=k: e.dma_start(
                    a, dd["wo"][k * 128:(k + 1) * 128, :]),
                    waits=[wo_ready[k], prev_swo.get(k % 2), xn_tok],
                    inc="LD", inc_n=16)
                if DBG_L1A and tile == 0 and wv == 0 and k == 0 and not g.dbg_done7:
                    d7 = em.op("gpsimd", lambda e, a=slot[:, 0:512]: e.dma_start(
                        g.d["dbg"][:, 1536:2048], a), waits=[lt], inc="LD", inc_n=16)
                    d8 = em.op("gpsimd", lambda e, a=ar(g, A1_Y, 512): e.dma_start(
                        g.d["dbg"][:, 2304:2816], a), waits=[lt], inc="LD", inc_n=16)
                    y_done = d8 if False else y_done
                    g.dbg_done7 = (d7, d8)
                for m4 in range(4):
                    m = wv * 4 + m4
                    w = []
                    if k == 0 and m4 == 0:
                        w = [y_done, wave_evac[0], lt, prev_wo_evac]
                        if DBG_L1A and g.dbg_done7 and tile == 0:
                            w += list(g.dbg_done7)
                    elif m4 == 0:
                        w = [lt]
                    tk_pe = mm(g.ps_big[:, m4 * 512: m4 * 512 + 512],
                               slot[:, m * 128: m * 128 + 128],
                               ar(g, A1_Y + k * 512, 512),
                               start=(k == 0), stop=(k == C8 - 1),
                               waits=w, inc="PE" if m4 == 3 else None)
                prev_swo[k % 2] = tk_pe
            if DBG_L1A and tile == 0 and wv == 0 and not g.dbg_done6:
                dtk = em.op("scalar", lambda e: e.copy(ar(g, A1_KSQ, 512), g.ps_big[:, 0:512]),
                            waits=[em.tok("PE")], inc="ACT")
                em.op("gpsimd", lambda e: e.dma_start(g.d["dbg"][:, 1024:1536], ar(g, A1_KSQ, 512)),
                      waits=[dtk], inc="LD", inc_n=16)
                em.wait("vector", [dtk])
                em.wait("tensor", [dtk])
                g.dbg_done6 = True
            for m4 in range(4):
                m = wv * 4 + m4
                prev_wo_evac = em.op("vector", lambda e, m=m, m4=m4, tw=tw: e.scalar_tensor_tensor(
                    X[:, m * T + tw: m * T + tw + 512],
                    g.ps_big[:, m4 * 512: m4 * 512 + 512], 1.0,
                    X[:, m * T + tw: m * T + tw + 512], ALU.mult, ALU.add),
                    waits=[em.tok("PE")] if m4 == 0 else (), inc="DVE")
            if DBG_BAR:
                em.barrier()
        if DBG_L1A and tile == 0 and not g.dbg_done5:
            em.op("gpsimd", lambda e: e.dma_start(g.d["dbg"][:, 3072:4096], dd["woc"][0]),
                  waits=[wo_ready[0]], inc="LD", inc_n=16)
            g.dbg_done5 = True
        if DBG_L1A and tile == 3 and not g.dbg_done8:
            em.op("gpsimd", lambda e, a=g.X[:, 0:512].bitcast(BF16):
                  e.dma_start(g.d["dbg"].bitcast(BF16)[:, 4096:5120], a),
                  waits=[prev_wo_evac], inc="LD", inc_n=16)
            g.dbg_done8 = True
        if DBG_L1A and tile == 0 and not g.dbg_done4:
            em.op("gpsimd", lambda e, a=g.X[:, 0:512].bitcast(BF16):
                  e.dma_start(g.d["dbg"].bitcast(BF16)[:, 0:1024], a),
                  waits=[prev_wo_evac], inc="LD", inc_n=16)
            g.dbg_done4 = True
    g.arena_free = prev_wo_evac
    g.ln_ps_free = prev_wo_evac
    return prev_wo_evac



A2_W1, A2_W2, A2_XG, A2_XGF, A2_H, A2_YF = 0, 4096, 8192, 9216, 10240, 10752
A2_YTM, A2_XT, A2_YT, A2_B1R, A2_B2R = 11776, 12800, 13824, 14848, 15360
A2_MSK, A2_TMP, A2_SLOT, A2_MSKF = 18432, 19456, 19968, 20480


def emit_l1ffn(g, xn_tok):
    em = g.em
    mm = g.mm
    dd = g.d
    X, XN = g.X, g.XN
    g.l1y_free = None
    g.l1yt_free = None
    psb2 = g.ps_acc[:].bitcast(BF16)
    tmpf = g.AR[:, A2_TMP: A2_TMP + 512].bitcast(F32)
    sloti = g.AR[:, A2_SLOT: A2_SLOT + 64].bitcast(I32)

    if DBG_L1F:
        em.op("gpsimd", lambda e: e.dma_start(g.d["dbg"][:, 1536:1792], g.routert[:]),
              waits=[xn_tok], inc="LD", inc_n=16)
        em.op("gpsimd", lambda e, a=g.X[:, 0:1024].bitcast(BF16):
              e.dma_start(g.d["dbg"].bitcast(BF16)[:, 8192:10240], a),
              waits=[xn_tok], inc="LD", inc_n=16)
        em.op("gpsimd", lambda e, a=g.XN[:, 0:1024]:
              e.dma_start(g.d["dbg"][:, 5120:6144], a),
              waits=[xn_tok], inc="LD", inc_n=16)
        for ci in range(8):
            em.op("gpsimd", lambda e, ci=ci: e.dma_start(
                g.d["dbg"][:, 2048 + ci * 128: 2048 + ci * 128 + 128],
                g.XN[:, ci * T: ci * T + 128]), waits=[xn_tok], inc="LD", inc_n=16)
    lg_slots = [g.ps_s1, g.ps_s2]
    msk_toks = []
    prev_msk = {0: g.arena_free, 1: g.arena_free}
    for tc in range(16):
        pb = lg_slots[tc % 2][:, 0:32]
        for ci in range(C8):
            mm(pb, XN[:, ci * T + tc * 128: ci * T + tc * 128 + 128],
               g.routert[:, ci * 32: ci * 32 + 32],
               start=(ci == 0), stop=False,
               waits=[xn_tok, prev_msk[tc % 2]] if ci == 0 else ())
        tk_pe = mm(pb, g.ones_col[:], g.rbrow[:], start=False, stop=True, inc="PE")
        if DBG_L1F and tc == 0:
            dtk0 = em.op("scalar", lambda e, c=pb: e.copy(tmpf[:, 224:256], c),
                         waits=[tk_pe], inc="ACT")
            dmp0 = em.op("gpsimd", lambda e: e.dma_start(
                g.d["dbg"][:, 1090:1122], tmpf[:, 224:256]), waits=[dtk0], inc="LD", inc_n=16)
            em.wait("vector", [dmp0])
        m1 = em.op("vector", lambda e, a=tmpf[:, 96:97], b=pb: e.reduce_max(a, b, axis=AX),
                   waits=[tk_pe, em.tok("DVE")], inc="DVE")
        if DBG_L1F and tc == 0:
            em.op("gpsimd", lambda e: e.dma_start(
                g.d["dbg"][:, 1122:1124], tmpf[:, 96:98]), waits=[m1], inc="LD", inc_n=16)
        m2 = em.op("vector", lambda e, a=tmpf[:, 0:32], b=pb, c=tmpf[:, 96:97]:
                   e.tensor_single_scalar(a, b, c, ALU.is_ge), waits=[m1], inc="DVE")
        em.op("vector", lambda e, a=tmpf[:, 32:64], b=tmpf[:, 0:32]:
              e.tensor_tensor_scan(a, b, b, 0.0, ALU.add, ALU.bypass),
              waits=[em.tok("DVE")], inc="DVE")
        em.op("vector", lambda e, a=tmpf[:, 64:96], b=tmpf[:, 32:64]:
              e.tensor_single_scalar(a, b, 1.0, ALU.is_equal),
              waits=[em.tok("DVE")], inc="DVE")
        mskf = g.AR[:, A2_MSKF: A2_MSKF + 1024].bitcast(F32)
        em.op("vector", lambda e, a=mskf[:, tc * 32: tc * 32 + 32],
              b=tmpf[:, 0:32], c=tmpf[:, 64:96]:
              e.tensor_tensor(a, b, c, ALU.mult),
              waits=[em.tok("DVE")], inc="DVE")
        tk = em.op("vector", lambda e, a=ar(g, A2_MSK + tc * 32, 32),
                   b=mskf[:, tc * 32: tc * 32 + 32]:
                   e.tensor_copy(a, b), waits=[em.tok("DVE")], inc="DVE")
        if DBG_L1F and tc == 0:
            em.op("gpsimd", lambda e: e.dma_start(
                g.d["dbg"][:, 1152:1248], tmpf[:, 0:96]), waits=[tk], inc="LD", inc_n=16)
            em.op("gpsimd", lambda e, m=mskf: e.dma_start(
                g.d["dbg"][:, 1248:1280], m[:, 0:32]), waits=[tk], inc="LD", inc_n=16)
            em.op("gpsimd", lambda e: e.dma_start(
                g.d["dbg"][:, 1280:1312], g.AR[:, A2_MSK: A2_MSK + 32]), waits=[tk], inc="LD", inc_n=16)
            em.wait("vector", [em.tok("LD")])
        prev_msk[tc % 2] = tk
        msk_toks.append(tk)

    slot_toks = []
    prev_cum = {0: None, 1: None}
    for tc in range(16):
        pb = lg_slots[tc % 2][:, 64:96]
        for sc in range(tc + 1):
            lhs = g.triu128[:] if sc == tc else g.ones128[:]
            mm(pb, lhs, ar(g, A2_MSK + sc * 32, 32),
               start=(sc == 0), stop=False,
               waits=[msk_toks[-1], prev_cum[tc % 2]] if sc == 0 else ())
        tk_pe = mm(pb, g.ones_col[:], g.eoffrow[:], start=False, stop=True, inc="PE")
        if DBG_L1F and tc == 0:
            dtk = em.op("scalar", lambda e, c=pb: e.copy(tmpf[:, 192:224], c),
                        waits=[tk_pe], inc="ACT")
            em.op("gpsimd", lambda e: e.dma_start(
                g.d["dbg"][:, 1024:1056], tmpf[:, 192:224]), waits=[dtk], inc="LD", inc_n=16)
        mskf = g.AR[:, A2_MSKF: A2_MSKF + 1024].bitcast(F32)
        a1 = em.op("vector", lambda e, a=tmpf[:, 128:160], b=mskf[:, tc * 32: tc * 32 + 32], c=pb:
                   e.tensor_tensor(a, c, b, ALU.mult), waits=[tk_pe, em.tok("DVE")], inc="DVE")
        if DBG_L1F and tc == 0:
            em.op("gpsimd", lambda e: e.dma_start(
                g.d["dbg"][:, 1056:1088], tmpf[:, 128:160]), waits=[a1], inc="LD", inc_n=16)
        a2 = em.op("vector", lambda e, a=tmpf[:, 160:161], b=tmpf[:, 128:160]:
                   e.reduce_sum(a, b, axis=AX), waits=[a1], inc="DVE")
        prev_cum[tc % 2] = a2
        a3 = em.op("vector", lambda e, a=tmpf[:, 160:161]:
                   e.tensor_scalar_add(a, a, -1.0), waits=[a2], inc="DVE")
        a4 = em.op("vector", lambda e, a=tmpf[:, 161:162], b=tmpf[:, 160:161]:
                   e.tensor_single_scalar(a, b, 0.0, ALU.is_lt), waits=[a3], inc="DVE")
        a5 = em.op("vector", lambda e, a=tmpf[:, 160:161], b=tmpf[:, 161:162]:
                   e.scalar_tensor_tensor(a, b, 4097.0, a, ALU.mult, ALU.add),
                   waits=[a4], inc="DVE")
        a5 = em.op("vector", lambda e, a=tmpf[:, 160:161]:
                   e.tensor_scalar_min(a, a, 4096.0), waits=[a5], inc="DVE")
        if DBG_L1F and tc == 0:
            em.op("gpsimd", lambda e: e.dma_start(
                g.d["dbg"][:, 1088:1090], tmpf[:, 160:162]), waits=[a5], inc="LD", inc_n=16)
        em.op("vector", lambda e, a=sloti[:, tc:tc + 1], b=tmpf[:, 160:161]:
              e.tensor_copy(a, b), waits=[a5], inc="DVE")
        slot_toks.append(em.op("vector", lambda e, a=tmpf[:, 162:163], b=sloti[:, tc:tc + 1]:
                               e.tensor_copy(a, b), waits=[em.tok("DVE")], inc="DVE"))

    if DBG_L1F:
        em.op("gpsimd", lambda e: e.dma_start(
            g.d["dbg"].bitcast(BF16)[:, 0:64], g.AR[:, A2_SLOT: A2_SLOT + 64]),
            waits=[slot_toks[-1]], inc="LD", inc_n=16)
        em.op("gpsimd", lambda e: e.dma_start(
            g.d["dbg"][:, 512:1024], ar(g, A2_MSK, 512)),
            waits=[slot_toks[-1]], inc="LD", inc_n=16)

    prev_ph2 = None
    scat_toks = []
    for tc in (range(0) if SKIP_L1F_P23 else range(16)):
        for j in range(C8):
            tk_pe = em.op("tensor", lambda t, o=psb2[:, j * 128: j * 128 + 128],
                          i_=XN[:, j * T + tc * 128: j * T + tc * 128 + 128]:
                          t.transpose(o, i_, g.ident[:]),
                          waits=[slot_toks[tc], prev_ph2] if j == 0 else (),
                          inc="PE" if j == C8 - 1 else None)
        ev = em.op("vector", lambda e, a=ar(g, A2_XT, 1024), b=psb2[:, 0:1024]:
                   e.tensor_copy(a, b), waits=[tk_pe], inc="DVE")
        sc = em.op("gpsimd", lambda e, i=sloti[:, tc:tc + 1]: e.indirect_dma_start(
            dd["xg_dram"][0:128, :], IndirectOffsetOnAxis(i, 0),
            ar(g, A2_XT, 1024), None),
            waits=[ev], inc="LD", inc_n=16)
        prev_ph2 = sc
        scat_toks.append(sc)

    slot_pe = {0: None, 1: None}
    slot_ld = {0: None, 1: None}
    store_toks = []
    prev_gather = None
    prev_ytr = None
    for e in (range(0) if SKIP_L1F_P23 else range(32)):
        s = e % 2
        lw1 = em.op("gpsimd", lambda e_, x=e, s=s: e_.dma_start(
            ar(g, A2_W1 + s * 2048, 2048).rearrange("p (c h) -> p c h", c=8),
            dd["w1L1"][x].rearrange("(c p) h -> p c h", p=128)),
            waits=[slot_pe[s]], inc="LD", inc_n=16)
        lw2 = em.op("gpsimd", lambda e_, x=e, s=s: e_.dma_start(
            ar(g, A2_W2 + s * 2048, 2048).rearrange("p (c h) -> p c h", c=2),
            dd["w2L1"][x].rearrange("(c p) h -> p c h", p=128)), inc="LD", inc_n=16)
        lb1 = em.op("gpsimd", lambda e_, x=e, s=s: e_.dma_start(
            g.AR[0:1, A2_B1R + s * 256: A2_B1R + s * 256 + 256], dd["b1L1r"][x:x + 1, :]),
            inc="LD", inc_n=16)
        lb2 = em.op("gpsimd", lambda e_, x=e, s=s: e_.dma_start(
            g.AR[0:1, A2_B2R + s * 1024: A2_B2R + s * 1024 + 1024], dd["b2L1r"][x:x + 1, :]),
            inc="LD", inc_n=16)
        slot_ld[s] = lb2
        gt = em.op("sync", lambda e_, x=e: e_.dma_start(
            ar(g, A2_XG, 1024), dd["xg_dram"][x * 128:(x + 1) * 128, :]),
            waits=[scat_toks[-1], prev_gather], inc="IO", inc_n=16)
        for j in range(C8):
            tk_pe = em.op("tensor", lambda t, o=psb2[:, j * 128: j * 128 + 128],
                          i_=g.AR[:, A2_XG + j * 128: A2_XG + j * 128 + 128]:
                          t.transpose(o, i_, g.ident[:]),
                          waits=[gt, prev_ytr] if j == 0 else (),
                          inc="PE" if j == C8 - 1 else None)
        xgf_ev = em.op("vector", lambda e_, a=ar(g, A2_XGF, 1024), b=psb2[:, 0:1024]:
                       e_.tensor_copy(a, b), waits=[tk_pe], inc="DVE")
        prev_gather = xgf_ev
        h_evs = []
        for mh in range(2):
            hp = [g.ps_s1, g.ps_s2][mh][:, 128:256]
            for ci in range(C8):
                mm(hp, ar(g, A2_W1 + s * 2048 + ci * 256 + mh * 128, 128),
                   ar(g, A2_XGF + ci * 128, 128),
                   start=(ci == 0), stop=False,
                   waits=[lw1, xgf_ev, h_evs[mh - 2] if mh >= 2 else None] if ci == 0 else ())
            tk_pe = mm(hp, g.AR[0:1, A2_B1R + s * 256 + mh * 128: A2_B1R + s * 256 + mh * 128 + 128],
                       g.ones_row[0:1, 0:128], start=False, stop=True,
                       waits=[lb1] if mh == 0 else (), inc="PE")
            h_evs.append(em.op("scalar", lambda e_, a=ar(g, A2_H + mh * 128, 128), b=hp:
                               e_.activation(a, b, AF.Relu), waits=[tk_pe], inc="ACT"))
        for m in range(C8):
            pb = g.ps_big[:, m * 128: m * 128 + 128]
            for kh in range(2):
                mm(pb, ar(g, A2_W2 + s * 2048 + kh * 1024 + m * 128, 128),
                   ar(g, A2_H + kh * 128, 128),
                   start=(kh == 0), stop=False,
                   waits=[h_evs[-1], g.l1y_free, lw2, lb2] if (m == 0 and kh == 0) else ())
            tk_pe = mm(pb, g.AR[0:1, A2_B2R + s * 1024 + m * 128: A2_B2R + s * 1024 + m * 128 + 128],
                       g.ones_row[0:1, 0:128], start=False, stop=True,
                       inc="PE" if m == C8 - 1 else None)
        slot_pe[s] = tk_pe
        yf_ev = em.op("scalar", lambda e_, a=ar(g, A2_YF, 1024), b=g.ps_big[:, 0:1024]:
                      e_.activation(a, b, AF.Relu), waits=[tk_pe], inc="ACT")
        for j in range(C8):
            tk_pe = em.op("tensor", lambda t, o=psb2[:, 1024 + j * 128: 1024 + j * 128 + 128],
                          i_=g.AR[:, A2_YF + j * 128: A2_YF + j * 128 + 128]:
                          t.transpose(o, i_, g.ident[:]),
                          waits=[yf_ev, g.l1yt_free] if j == 0 else (),
                          inc="PE" if j == C8 - 1 else None)
        ytm_ev = em.op("vector", lambda e_, a=ar(g, A2_YTM, 1024), b=psb2[:, 1024:2048]:
                       e_.tensor_copy(a, b), waits=[tk_pe], inc="DVE")
        prev_ytr = ytm_ev
        g.l1y_free = yf_ev
        st = em.op("sync", lambda e_, x=e: e_.dma_start(
            dd["yg_dram"][x * 128:(x + 1) * 128, :], ar(g, A2_YTM, 1024)),
            waits=[ytm_ev], inc="IO", inc_n=16)
        g.l1yt_free = st
        store_toks.append(st)

    prev_add = None
    prev_g4 = None
    for tc in (range(0) if SKIP_L1F_P4 else range(16)):
        gt = em.op("gpsimd", lambda e, i=sloti[:, tc:tc + 1]: e.indirect_dma_start(
            ar(g, A2_YT, 1024), None,
            dd["yg_dram"][0:128, :], IndirectOffsetOnAxis(i, 0)),
            waits=[store_toks[-1], prev_g4], inc="LD", inc_n=16)
        for j in range(C8):
            tk_pe = em.op("tensor", lambda t, o=psb2[:, j * 128: j * 128 + 128],
                          i_=g.AR[:, A2_YT + j * 128: A2_YT + j * 128 + 128]:
                          t.transpose(o, i_, g.ident[:]),
                          waits=[gt, prev_add] if j == 0 else (),
                          inc="PE" if j == C8 - 1 else None)
        xap = X[:].rearrange("p (c t) -> p c t", c=C8)[:, :, tc * 128: tc * 128 + 128]
        prev_add = em.op("vector", lambda e, a=xap, b=psb2[:, 0:1024].rearrange("p (c q) -> p c q", c=8):
                         e.scalar_tensor_tensor(a, b, 0.5, a, ALU.mult, ALU.add),
                         waits=[tk_pe], inc="DVE")
        prev_g4 = prev_add
    if SKIP_L1F_P4:
        prev_add = store_toks[-1] if store_toks else slot_toks[-1]
    g.arena_free = prev_add
    g.ln_ps_free = prev_add
    return prev_add




DBG_L1A = False
DBG_L1F = False
DBG_BAR = False
SKIP_L1F_P4 = False
SKIP_L1F_P23 = False
SW2DIRECT = True


def build_program(parts):
    nc = bass.Bass("TRN2", target_bir_lowering=False)
    g = G()
    g.nc = nc
    g.em = Em()
    g.mm = None
    em = g.em
    for s in ("LD", "IO", "PE", "ACT", "DVE", "BAR"):
        em.sem(s)

    def mm(out_ap, lhsT_ap, rhs_ap, start, stop, waits=(), inc=None):
        return em.op(
            "tensor",
            lambda t, o=out_ap, l=lhsT_ap, r=rhs_ap, s1=start, s2=stop: t.matmul(
                o, l, r, start=s1, stop=s2),
            waits=waits, inc=inc)
    g.mm = mm

    g.d = declare_inputs(nc, parts)
    alloc_state(g, parts)
    g.sw2_cached = False
    g.wo_cached = False
    g.dbg_done = False
    g.dbg_done2 = False
    g.dbg_done3 = False
    g.dbg_sct = {}
    g.dbg_done4 = False
    g.dbg_done5 = False
    g.dbg_done6 = False
    g.dbg_done7 = None
    g.dbg_done8 = False

    c_tok = emit_load_consts(g, parts)
    g.arena_free = None
    g.ln_ps_free = None
    for seq in range(NSEQ):
        x_tok = em.op("sync", lambda e, seq=seq: e.dma_start(
            g.X[:].rearrange("p (c t) -> p c t", c=8), g.d["xT"][seq].rearrange("(c p) t -> p c t", p=128)),
            inc="IO", inc_n=16)
        g.ln_pre = x_tok
        em.wait("vector", [c_tok])
        last = None
        for sl in parts:
            xn_tok = emit_ln(g, g.ln_pre)
            if sl == "l0ffn":
                last = emit_l0ffn(g, xn_tok)
            elif sl == "l0attn":
                last = emit_l0attn(g, xn_tok)
            elif sl == "l1attn":
                last = emit_l1attn(g, xn_tok)
            elif sl == "l1ffn":
                last = emit_l1ffn(g, xn_tok)
            em.barrier()
            g.ln_pre = None
            g.arena_free = None
            g.ln_ps_free = None
        st = em.op("sync", lambda e, seq=seq: e.dma_start(
            g.d["outT"][seq].rearrange("(c p) t -> p c t", p=128), g.X[:].rearrange("p (c t) -> p c t", c=8)),
            inc="IO", inc_n=16)
        em.wait("sync", [st])
        em.barrier()
    em.build(nc)
    return g




def make_consts():
    c = {}
    c["ident"] = np.eye(128, dtype=np.float32)
    c["ones_col"] = np.ones((1, 128), np.float32)
    c["ones_red"] = np.ones((128, 1), np.float32)
    c["ones_row"] = np.ones((128, 512), np.float32)
    return c


def fm(v, nch):
    return np.ascontiguousarray(v.reshape(nch, 128).T)


def prep_inputs(parts, inputs):
    base = make_consts()
    if "l0ffn" in parts:
        base["sw1"] = inputs["L0_ffn_sw1"]
        base["sw3"] = inputs["L0_ffn_sw3"]
        base["sw2"] = inputs["L0_ffn_sw2"]
        base["sb1fm"] = fm(np.asarray(inputs["L0_ffn_sb1"]), 16)
        base["sb3fm"] = fm(np.asarray(inputs["L0_ffn_sb3"]), 16)
        rp = np.zeros((128, 3072), np.float32)
        rp[64, 2048:3072] = np.asarray(inputs["L0_ffn_sb2"])
        b2 = np.asarray(inputs["L0_ffn_b2"])
        for e in range(8):
            rp[32 * (e % 3), (e // 3) * 1024:(e // 3 + 1) * 1024] = b2[e]
        base["rowpack"] = rp
        base["w1"] = inputs["L0_ffn_w1"]
        base["w2"] = inputs["L0_ffn_w2"]
        base["b1fm"] = np.ascontiguousarray(
            np.asarray(inputs["L0_ffn_b1"]).reshape(8, 8, 128).transpose(2, 0, 1).reshape(128, 64))
    if "l0attn" in parts:
        base["wq"] = inputs["L0_attn_Wq"]; base["wk"] = inputs["L0_attn_Wk"]
        base["wv"] = inputs["L0_attn_Wv"]
    if "l1attn" in parts:
        base["wqkv"] = inputs["L1_attn_W_qkv"]; base["wo"] = inputs["L1_attn_W_o"]
        hs = np.zeros((128, 128), np.float32)
        ht = np.zeros((16, 1024), np.float32)
        for cc in range(8):
            for h2 in range(2):
                h = 2 * cc + h2
                hs[64 * h2:64 * h2 + 64, cc * 16 + h] = 1.0
                ht[h, cc * 128 + 64 * h2: cc * 128 + 64 * h2 + 64] = 1.0
        base["hsel16"] = hs; base["hselT"] = ht
    if ("l1attn" in parts) or ("l1ffn" in parts):
        base["triu128"] = np.triu(np.ones((128, 128), np.float32))
    if "l1ffn" in parts:
        base["routerT"] = np.ascontiguousarray(np.asarray(inputs["L1_ffn_router_w"]).T)
        base["rbrow"] = np.asarray(inputs["L1_ffn_router_b"]).reshape(1, 32)
        base["w1L1"] = inputs["L1_ffn_weights1"]
        base["w2L1"] = inputs["L1_ffn_weights2"]
        base["b1L1r"] = np.asarray(inputs["L1_ffn_bias1"], dtype=np.float32)
        base["b2L1r"] = np.asarray(inputs["L1_ffn_bias2"], dtype=np.float32)
        base["eoffrow"] = (np.arange(32, dtype=np.float32) * 128).reshape(1, 32)
        base["ones128"] = np.ones((128, 128), np.float32)
    return base


LAST_DBG = None


def run_parts(parts, inputs, x):
    g = build_program(parts)
    base = prep_inputs(parts, inputs)
    xT = np.ascontiguousarray(x.transpose(0, 2, 1)).astype(np.float32)
    in_maps = []
    for c in range(NCORE):
        m = dict(base)
        m["xT"] = np.ascontiguousarray(xT[2 * c: 2 * c + 2])
        in_maps.append(m)
    res = run_bass_kernel_spmd(g.nc, in_maps, core_ids=list(range(NCORE)))
    global LAST_DBG
    LAST_DBG = np.asarray(res.results[0].get("dbg")) if "dbg" in res.results[0] else None
    out = np.concatenate([r["outT"] for r in res.results], axis=0)
    return np.ascontiguousarray(
        out.reshape(NCORE * NSEQ, D, T).transpose(0, 2, 1))


def kernel(**inputs):
    _ = np.asarray(inputs["L0_ffn_t_emb"])
    x = np.asarray(inputs["L0_attn_x"], dtype=np.float32)
    return run_parts(["l0ffn", "l0attn", "l0ffn", "l1ffn", "l1attn", "l1ffn"],
                     inputs, x)


# version 08 (final)
# speedup vs baseline: 1.0292x; 1.0292x over previous
import contextlib
import numpy as np
import concourse.bass as bass
import concourse.mybir as mybir
from concourse.bass_utils import run_bass_kernel_spmd
from concourse.bass_core_rust import IndirectOffsetOnAxis

F32 = mybir.dt.float32
BF16 = mybir.dt.bfloat16
I32 = mybir.dt.int32
U32 = mybir.dt.uint32
AF = mybir.ActivationFunctionType
ALU = mybir.AluOpType
AX = mybir.AxisListType.X

D = 1024
T = 2048
C8 = 8
NSEQ = 2
NCORE = 8
AR_ELEMS = 48640

ENGINES = ("tensor", "vector", "scalar", "gpsimd", "sync")

A_SW1, A_SW3, A_GB, A_SWK, A_SILU = 0, 16384, 32768, 40960, 43008
A_SQ, A_RB, A_MB = 43520, 47616, 48128
A_EW1, A_EW2, A_XR, A_HB = 0, 16384, 32768, 34816
A_KFM, A_VTM, A_WQ, A_ATT, A_ATS, A_QBF = 0, 16384, 32768, 40960, 45056, 47104


class Em:
    def __init__(self):
        self.ops = {e: [] for e in ENGINES}
        self.counts = {}
        self.sems = {}

    def sem(self, name):
        self.counts[name] = 0

    def wait(self, eng, toks):
        best = {}
        for tk in toks:
            if tk is None:
                continue
            n, v = tk
            if v > best.get(n, -1):
                best[n] = v
        for n, v in sorted(best.items()):
            if v > 0:
                self.ops[eng].append(lambda e, n=n, v=v: e.wait_ge(self.sems[n], v))

    def op(self, eng, builder, waits=(), inc=None, inc_n=1):
        self.wait(eng, waits)
        tok = None
        if inc is not None:
            self.counts[inc] += inc_n
            tok = (inc, self.counts[inc])
            self.ops[eng].append(
                lambda e, b=builder, s=inc, n=inc_n: b(e).then_inc(self.sems[s], n))
        else:
            self.ops[eng].append(lambda e, b=builder: b(e))
        return tok

    def tok(self, sem):
        return (sem, self.counts[sem])

    def barrier(self):
        self.counts["BAR"] += len(ENGINES)
        tgt = self.counts["BAR"]
        for eng in ENGINES:
            if eng == "vector":
                self.ops[eng].append(
                    lambda e: e.tensor_copy(self.bsc[:, 0:2], self.bsc[:, 2:4])
                    .then_inc(self.sems["BAR"], 1))
            elif eng == "scalar":
                self.ops[eng].append(
                    lambda e: e.copy(self.bsc[:, 4:6], self.bsc[:, 6:8])
                    .then_inc(self.sems["BAR"], 1))
            else:
                self.ops[eng].append(lambda e: e.nop().then_inc(self.sems["BAR"], 1))
            self.ops[eng].append(lambda e, v=tgt: e.wait_ge(self.sems["BAR"], v))

    def build(self, nc):
        with contextlib.ExitStack() as st:
            for name in self.counts:
                self.sems[name] = st.enter_context(nc.semaphore(name))
            block = st.enter_context(nc.Block())
            for eng in ENGINES:
                getattr(block, eng)(lambda e, eng=eng: [f(e) for f in self.ops[eng]])


class G:
    pass


def ar(g, off, n):
    return g.AR[:, off: off + n]




def declare_inputs(nc, parts):
    d = {}

    def inp(name, shape, dtype=F32):
        d[name] = nc.dram_tensor(name, list(shape), dtype, kind="ExternalInput").ap()

    inp("xT", (NSEQ, D, T))
    d["outT"] = nc.dram_tensor("outT", [NSEQ, D, T], F32, kind="ExternalOutput").ap()
    d["dbg"] = nc.dram_tensor("dbg", [128, 8192], F32, kind="ExternalOutput").ap()
    inp("ident", (128, 128))
    inp("ones_col", (1, 128))
    inp("ones_red", (128, 1))
    inp("ones_row", (128, 512))
    if "l0ffn" in parts:
        inp("sw1", (D, 2048)); inp("sw3", (D, 2048)); inp("sw2", (2048, D))
        inp("sb1fm", (128, 16)); inp("sb3fm", (128, 16))
        inp("rowpack", (128, 3072))
        inp("w1", (8, D, D)); inp("w2", (8, D, D))
        inp("b1fm", (128, 64))
        d["sw2c"] = nc.dram_tensor("sw2c", [16, 128, D], BF16, kind="Internal").ap()
    if "l0attn" in parts:
        inp("wq", (D, D)); inp("wk", (D, D)); inp("wv", (D, D))
    if "l1attn" in parts:
        inp("wqkv", (D, 3 * D)); inp("wo", (D, D))
        inp("hsel16", (128, 128)); inp("hselT", (16, 1024))
        d["woc"] = nc.dram_tensor("woc", [8, 128, D], BF16, kind="Internal").ap()
    if ("l1attn" in parts) or ("l1ffn" in parts):
        inp("triu128", (128, 128))
    if "l1ffn" in parts:
        inp("routerT", (D, 32)); inp("rbrow", (1, 32))
        inp("w1L1", (32, D, 256)); inp("w2L1", (32, 256, D))
        inp("b1L1r", (32, 256)); inp("b2L1r", (32, D))
        inp("eoffrow", (1, 32)); inp("ones128", (128, 128))
        d["xg_dram"] = nc.dram_tensor("xg_dram", [32 * 128 + 8, D], BF16, kind="Internal").ap()
        d["yg_dram"] = nc.dram_tensor("yg_dram", [32 * 128 + 8, D], BF16, kind="Internal").ap()
    return d


def alloc_state(g, parts):
    nc = g.nc
    g.X = nc.alloc_sbuf_tensor("X", [128, C8 * T], F32)
    g.XN = nc.alloc_sbuf_tensor("XN", [128, C8 * T], BF16)
    g.AR = nc.alloc_sbuf_tensor("AR", [128, AR_ELEMS], BF16)
    g.ident = nc.alloc_sbuf_tensor("identb", [128, 128], BF16)
    g.ones_col = nc.alloc_sbuf_tensor("ones_colb", [1, 128], BF16)
    g.ones_red = nc.alloc_sbuf_tensor("ones_redb", [128, 1], BF16)
    g.ones_row = nc.alloc_sbuf_tensor("ones_rowb", [128, 512], BF16)
    if "l0ffn" in parts:
        g.sb1fm = nc.alloc_sbuf_tensor("sb1fm_sb", [128, 16], F32)
        g.sb3fm = nc.alloc_sbuf_tensor("sb3fm_sb", [128, 16], F32)
        g.rowpack = nc.alloc_sbuf_tensor("rowpackb", [128, 3072], BF16)
        g.b1fm = nc.alloc_sbuf_tensor("b1fm_sb", [128, 64], F32)
    if "l0attn" in parts:
        g.sm1 = nc.alloc_sbuf_tensor("sm1", [128, 8], F32)
    if "l1attn" in parts:
        g.hsel16 = nc.alloc_sbuf_tensor("hsel16b", [128, 128], BF16)
        g.hselT = nc.alloc_sbuf_tensor("hselTb", [16, 1024], BF16)
    if ("l1attn" in parts) or ("l1ffn" in parts):
        g.triu128 = nc.alloc_sbuf_tensor("triu128b", [128, 128], BF16)
    if "l1ffn" in parts:
        g.rbrow = nc.alloc_sbuf_tensor("rbrowb", [1, 32], BF16)
        g.eoffrow = nc.alloc_sbuf_tensor("eoffrow_sb", [1, 32], BF16)
        g.ones128 = nc.alloc_sbuf_tensor("ones128b", [128, 128], BF16)
        g.routert = nc.alloc_sbuf_tensor("routert", [128, 8 * 32], BF16)
    g.em.bsc = nc.alloc_sbuf_tensor("bar_scratch", [128, 8], F32)
    g.ps_big = nc.alloc_psum_tensor("ps_big", [128, 2048], F32)
    g.ps_acc = nc.alloc_psum_tensor("ps_acc", [128, 1024], F32)
    g.ps_s1 = nc.alloc_psum_tensor("ps_s1", [128, 512], F32)
    g.ps_s2 = nc.alloc_psum_tensor("ps_s2", [128, 512], F32)


def emit_load_consts(g, parts):
    em = g.em
    dd = g.d

    def ld(dst, src):
        return em.op("gpsimd", lambda e, a=dst, b=src: e.dma_start(a, b),
                     inc="LD", inc_n=16)

    ld(g.ident[:], dd["ident"])
    ld(g.ones_col[:], dd["ones_col"])
    ld(g.ones_red[:], dd["ones_red"])
    tk = ld(g.ones_row[:], dd["ones_row"])
    if "l0ffn" in parts:
        ld(g.sb1fm[:], dd["sb1fm"]); ld(g.sb3fm[:], dd["sb3fm"])
        ld(g.rowpack[:], dd["rowpack"]); tk = ld(g.b1fm[:], dd["b1fm"])
    if "l1attn" in parts:
        ld(g.hsel16[:], dd["hsel16"]); tk = ld(g.hselT[:], dd["hselT"])
    if ("l1attn" in parts) or ("l1ffn" in parts):
        tk = ld(g.triu128[:], dd["triu128"])
    if "l1ffn" in parts:
        ld(g.rbrow[:], dd["rbrow"]); ld(g.eoffrow[:], dd["eoffrow"])
        ld(g.ones128[:], dd["ones128"])
        tk = ld(g.routert[:].rearrange("p (c e) -> p c e", c=8), dd["routerT"].rearrange("(c p) e -> p c e", p=128))
    return tk




def emit_ln(g, pre):
    em = g.em
    mm = g.mm
    X, XN = g.X, g.XN
    inv = 1.0 / D
    rrow = lambda c: g.AR[0:1, A_SQ + (c.start if isinstance(c, slice) else 0):
                          A_SQ + (c.stop if isinstance(c, slice) else T)]
    mrow = lambda c: g.AR[0:1, A_SQ + T + (c.start if isinstance(c, slice) else 0):
                          A_SQ + T + (c.stop if isinstance(c, slice) else T)]

    cast = []
    for ci in range(C8):
        tk = em.op("vector",
                   lambda e, a=XN[:, ci * T:(ci + 1) * T], b=X[:, ci * T:(ci + 1) * T]:
                   e.tensor_copy(a, b),
                   waits=[pre] if ci == 0 else (), inc="DVE")
        cast.append(tk)
    for ci in range(C8):
        for nt in range(4):
            tk_pe = mm(g.ps_big[0:1, nt * 512: nt * 512 + 512], g.ones_red[:],
                       XN[:, ci * T + nt * 512: ci * T + nt * 512 + 512],
                       start=(ci == 0), stop=(ci == C8 - 1),
                       waits=[cast[ci], g.ln_ps_free] if nt == 0 else (),
                       inc="PE" if (ci == C8 - 1 and nt == 3) else None)
    t1 = em.op("vector", lambda e, a=g.ps_big[0:1, 0:2048]: e.tensor_scalar_mul(a, a, inv),
               waits=[tk_pe], inc="DVE")
    t2 = em.op("scalar", lambda e, a=g.AR[0:1, A_SQ + T: A_SQ + 2 * T],
               b=g.ps_big[0:1, 0:2048]: e.copy(a, b), waits=[t1], inc="ACT")

    sub_done = []
    for nt in range(4):
        bc = mm(g.ps_s1[:, :], g.ones_col[:], mrow(slice(nt * 512, nt * 512 + 512)),
                start=True, stop=True,
                waits=[t2] + ([em.tok("DVE")] if nt else []), inc="PE")
        ev = em.op("scalar", lambda e, a=ar(g, A_MB, 512), b=g.ps_s1[:, :]:
                   e.copy(a, b), waits=[bc], inc="ACT")
        for ci in range(C8):
            cc = XN[:, ci * T + nt * 512: ci * T + nt * 512 + 512]
            tk = em.op("vector", lambda e, a=cc, b=ar(g, A_MB, 512):
                       e.tensor_sub(a, a, b),
                       waits=[ev] if ci == 0 else (), inc="DVE")
        sub_done.append(tk)

    sq_pe = {}
    for ci in range(C8):
        w = [sub_done[-1]]
        if ci >= 2:
            w.append(sq_pe[ci - 2])
        tk_sq = em.op("vector",
                      lambda e, a=ar(g, A_SQ + (ci % 2) * T, T), b=XN[:, ci * T:(ci + 1) * T]:
                      e.tensor_mul(a, b, b), waits=w, inc="DVE")
        for nt in range(4):
            sq_pe[ci] = mm(g.ps_big[0:1, nt * 512: nt * 512 + 512], g.ones_red[:],
                           ar(g, A_SQ + (ci % 2) * T + nt * 512, 512),
                           start=(ci == 0), stop=(ci == C8 - 1),
                           waits=[tk_sq, t2] if nt == 0 else (), inc="PE")
    v1 = em.op("vector", lambda e, a=g.ps_big[0:1, 0:2048]:
               e.tensor_scalar(a, a, inv, 1e-5, ALU.mult, ALU.add),
               waits=[sq_pe[C8 - 1]], inc="DVE")
    v2 = em.op("vector", lambda e, a=g.ps_big[0:1, 0:2048]: e.reciprocal(a, a),
               waits=[v1], inc="DVE")
    v3 = em.op("scalar", lambda e, a=g.AR[0:1, A_SQ: A_SQ + T],
               b=g.ps_big[0:1, 0:2048]: e.activation(a, b, AF.Sqrt),
               waits=[v2], inc="ACT")

    for nt in range(4):
        bc = mm(g.ps_s1[:, :], g.ones_col[:], rrow(slice(nt * 512, nt * 512 + 512)),
                start=True, stop=True, waits=[v3, em.tok("DVE")], inc="PE")
        ev = em.op("scalar", lambda e, a=ar(g, A_RB, 512), b=g.ps_s1[:, :]:
                   e.copy(a, b), waits=[bc], inc="ACT")
        for ci in range(C8):
            cc = XN[:, ci * T + nt * 512: ci * T + nt * 512 + 512]
            tk = em.op("vector", lambda e, a=cc, b=ar(g, A_RB, 512):
                       e.tensor_mul(a, a, b),
                       waits=[ev] if ci == 0 else (), inc="DVE")
    g.ln_ps_free = em.tok("DVE")
    return em.tok("DVE")




def emit_l0ffn(g, xn_tok):
    em = g.em
    mm = g.mm
    dd = g.d
    X, XN = g.X, g.XN

    t_sw1 = em.op("gpsimd", lambda e: e.dma_start(
        ar(g, A_SW1, 16384).rearrange("p (c h) -> p c h", c=8), dd["sw1"].rearrange("(c p) h -> p c h", p=128)),
        waits=[g.arena_free], inc="LD", inc_n=16)
    t_sw3 = em.op("gpsimd", lambda e: e.dma_start(
        ar(g, A_SW3, 16384).rearrange("p (c h) -> p c h", c=8), dd["sw3"].rearrange("(c p) h -> p c h", p=128)),
        inc="LD", inc_n=16)
    if SW2DIRECT:
        g.sw2_cached = True
        sw2_ready = [None] * 16
    elif not g.sw2_cached:
        g.sw2_cached = True
        st_toks = []
        for k in range(16):
            slot = ar(g, A_SWK + (k % 2) * D, D)
            w = [g.arena_free] + ([st_toks[k - 2]] if k >= 2 else [])
            ck = em.op("gpsimd", lambda e, a=slot, k=k: e.dma_start(
                a, dd["sw2"][k * 128:(k + 1) * 128, :]), waits=w, inc="LD", inc_n=16)
            st = em.op("sync", lambda e, a=slot, k=k: e.dma_start(dd["sw2c"][k], a),
                       waits=[ck], inc="IO", inc_n=16)
            st_toks.append(st)
        sw2_ready = [st_toks[-1]] * 16
    else:
        sw2_ready = [None] * 16

    prev_out_evac = None
    prev_swk = {}
    for tt in range(4):
        tw = slice(tt * 512, tt * 512 + 512)
        g_first = True
        for k in range(16):
            psA = g.ps_s1[:, :] if k % 2 == 0 else g.ps_big[:, 0:512]
            psB = g.ps_s2[:, :] if k % 2 == 0 else g.ps_big[:, 512:1024]
            for ci in range(C8):
                w = []
                if ci == 0:
                    if g_first:
                        w = [t_sw1, t_sw3, xn_tok] + ([prev_out_evac] if prev_out_evac else [])
                    else:
                        w = [g.gv_prev.get(k % 2)]
                mm(psA, ar(g, A_SW1 + ci * 2048 + k * 128, 128),
                   XN[:, ci * T + tt * 512: ci * T + tt * 512 + 512],
                   start=(ci == 0), stop=(ci == C8 - 1), waits=w)
            for ci in range(C8):
                tk_pe = mm(psB, ar(g, A_SW3 + ci * 2048 + k * 128, 128),
                           XN[:, ci * T + tt * 512: ci * T + tt * 512 + 512],
                           start=(ci == 0), stop=(ci == C8 - 1),
                           inc="PE" if ci == C8 - 1 else None)
            a = em.op("scalar", lambda e, k=k, pa=psA: e.activation(
                ar(g, A_SILU, 512), pa, AF.Silu, bias=g.sb1fm[:, k:k + 1]),
                waits=[tk_pe, g.gv_prev.get(1 - k % 2)], inc="ACT")
            v = em.op("vector", lambda e, k=k, pb=psB: e.scalar_tensor_tensor(
                ar(g, A_GB + k * 512, 512), pb, g.sb3fm[:, k:k + 1],
                ar(g, A_SILU, 512), ALU.add, ALU.mult),
                waits=[a], inc="DVE")
            g.gv_prev[k % 2] = v
            g_first = False
        g_done = em.tok("DVE")
        if tt == 3:
            g.sw13_free = em.tok("PE")

        outslot = [g.ps_big[:, j * 512:(j + 1) * 512] for j in range(4)] + \
                  [g.ps_acc[:, 0:512], g.ps_acc[:, 512:1024], g.ps_s1[:, :], g.ps_s2[:, :]]
        for co in range(C8):
            mm(outslot[co], g.rowpack[64:65, 2048 + co * 128: 2048 + co * 128 + 128],
               g.ones_row[64:65, :],
               start=True, stop=False, waits=[g_done] if co == 0 else ())
        for k in range(16):
            slot = ar(g, A_SWK + (k % 2) * D, D)
            if SW2DIRECT:
                lt = em.op("gpsimd", lambda e, a=slot, k=k: e.dma_start(
                    a, dd["sw2"][k * 128:(k + 1) * 128, :]),
                    waits=[sw2_ready[k], prev_swk.get(k % 2)], inc="LD", inc_n=16)
            else:
                lt = em.op("sync", lambda e, a=slot, k=k: e.dma_start(a, dd["sw2c"][k]),
                           waits=[sw2_ready[k], prev_swk.get(k % 2)], inc="IO", inc_n=16)
            for co in range(C8):
                tk_pe = mm(outslot[co], slot[:, co * 128: co * 128 + 128],
                           ar(g, A_GB + k * 512, 512),
                           start=False, stop=(k == 15),
                           waits=[lt] if co == 0 else (),
                           inc="PE" if co == C8 - 1 else None)
            prev_swk[k % 2] = tk_pe
        for co in [6, 7, 0, 1, 2, 3, 4, 5]:
            prev_out_evac = em.op("vector", lambda e, co=co, tt=tt: e.scalar_tensor_tensor(
                X[:, co * T + tt * 512: co * T + tt * 512 + 512], outslot[co], 0.25,
                X[:, co * T + tt * 512: co * T + tt * 512 + 512], ALU.mult, ALU.add),
                waits=[em.tok("PE")] if co == 6 else (), inc="DVE")

    shared_done = prev_out_evac

    order = [0, 5, 2, 7, 4, 1, 6, 3]
    pair = {r: (3 * r % 8, (3 * r + 7) % 8) for r in order}
    slot_of = {}

    def load_w1(e, s, waits=()):
        return em.op("gpsimd", lambda e_, x=e, s=s: e_.dma_start(
            ar(g, A_EW1 + s * 8192, 8192).rearrange("p (k h) -> p k h", k=8),
            dd["w1"][x].rearrange("(k p) h -> p k h", p=128)),
            waits=waits, inc="LD", inc_n=16)

    def load_w2(e, s, waits=()):
        return em.op("gpsimd", lambda e_, x=e, s=s: e_.dma_start(
            ar(g, A_EW2 + s * 8192, 8192).rearrange("p (k c) -> p k c", k=8),
            dd["w2"][x].rearrange("(k p) c -> p k c", p=128)),
            waits=waits, inc="LD", inc_n=16)

    e0, e1 = pair[order[0]]
    w = [shared_done, em.tok("PE")]
    slot_of[e0] = [0, load_w1(e0, 0, w), None]
    slot_of[e0][2] = load_w2(e0, 0)
    slot_of[e1] = [1, load_w1(e1, 1), None]
    slot_of[e1][2] = load_w2(e1, 1)

    moe_evac = None
    slot_act = {0: None, 1: None}
    for idx, r in enumerate(order):
        e0, e1 = pair[r]
        if idx + 1 < len(order):
            need = set(pair[order[idx + 1]])
            drop = e0 if e0 not in need else e1
            keep = e1 if drop == e0 else e0
            newe = next(iter(need - {keep}))
        else:
            drop, keep, newe = e0, e1, None
        gt = None
        for ci in range(C8):
            gt = em.op("vector", lambda e, ci=ci, r=r: e.tensor_copy(
                ar(g, A_XR + ci * 256, 256),
                XN[:, ci * T + r:(ci + 1) * T: 8]),
                waits=([xn_tok] + ([moe_evac] if moe_evac else [shared_done])) if ci == 0 else (),
                inc="DVE")
        h_pe_done = {}
        for eidx, e in enumerate((drop, keep)):
            s, tw1, tw2 = slot_of[e]
            for mh in range(8):
                pb = g.ps_acc[:, (mh % 2) * 512:(mh % 2) * 512 + 256]
                for ci in range(C8):
                    w = [tw1, gt, slot_act[mh % 2]] if ci == 0 else ()
                    tk_pe = mm(pb, ar(g, A_EW1 + s * 8192 + ci * 1024 + mh * 128, 128),
                               ar(g, A_XR + ci * 256, 256),
                               start=(ci == 0), stop=(ci == C8 - 1),
                               waits=w, inc="PE" if ci == C8 - 1 else None)
                slot_act[mh % 2] = em.op(
                    "scalar", lambda e_, x=e, mh=mh, ei=eidx, pb=pb: e_.activation(
                        ar(g, A_HB + ei * 2048 + mh * 256, 256), pb, AF.Gelu,
                        bias=g.b1fm[:, x * 8 + mh: x * 8 + mh + 1]),
                    waits=[tk_pe], inc="ACT")
            h_pe_done[e] = tk_pe
            if eidx == 0 and newe is not None:
                nw1 = load_w1(newe, slot_of[drop][0], waits=[tk_pe])
        h_done = em.tok("ACT")
        yslot = [g.ps_big[:, co * 256: co * 256 + 256] for co in range(C8)]
        for co in range(C8):
            for eidx, e in enumerate((drop, keep)):
                s, tw1, tw2 = slot_of[e]
                rp, rs = 32 * (e % 3), e // 3
                mm(yslot[co], g.rowpack[rp:rp + 1, rs * 1024 + co * 128: rs * 1024 + co * 128 + 128],
                   g.ones_row[rp:rp + 1, 0:256],
                   start=(eidx == 0), stop=False,
                   waits=([h_done, tw2] + ([moe_evac] if moe_evac else []))
                   if co == 0 else ())
                for k in range(8):
                    tk_pe = mm(yslot[co], ar(g, A_EW2 + s * 8192 + k * 1024 + co * 128, 128),
                               ar(g, A_HB + eidx * 2048 + k * 256, 256),
                               start=False, stop=(eidx == 1 and k == 7),
                               inc="PE" if (eidx == 1 and k == 7) else None)
        y_done = tk_pe
        if newe is not None:
            nw2 = load_w2(newe, slot_of[drop][0], waits=[y_done])
            slot_of[newe] = [slot_of[drop][0], nw1, nw2]
            del slot_of[drop]
        for co in range(C8):
            moe_evac = em.op("vector", lambda e, co=co, r=r: e.scalar_tensor_tensor(
                X[:, co * T + r:(co + 1) * T: 8], g.ps_big[:, co * 256: co * 256 + 256],
                0.125, X[:, co * T + r:(co + 1) * T: 8], ALU.mult, ALU.add),
                waits=[y_done] if co == 0 else (), inc="DVE")
    g.arena_free = moe_evac
    g.ln_ps_free = moe_evac
    return moe_evac




def emit_l0attn(g, xn_tok):
    em = g.em
    mm = g.mm
    dd = g.d
    X, XN = g.X, g.XN
    WB = lambda off, n: ar(g, A_WQ + off, n)
    psb = g.ps_acc[:].bitcast(BF16)

    def load_w(name, waits):
        return em.op("gpsimd", lambda e, nm=name: e.dma_start(
            WB(0, 8192).rearrange("p (k d) -> p k d", k=8),
            dd[nm].rearrange("(k p) d -> p k d", p=128)),
            waits=waits, inc="LD", inc_n=16)

    t_wk = load_w("wk", [g.arena_free])
    pslots = [g.ps_s1, g.ps_s2]
    ev_prev = {0: None, 1: None}
    for i, (dc, nt) in enumerate([(d, n) for d in range(C8) for n in range(4)]):
        pb = pslots[i % 2][:, :]
        for ci in range(C8):
            w = []
            if ci == 0:
                w = [t_wk if i == 0 else None, xn_tok if i == 0 else None,
                     ev_prev[i % 2]]
            tk_pe = mm(pb, WB(ci * 1024 + dc * 128, 128),
                       XN[:, ci * T + nt * 512: ci * T + nt * 512 + 512],
                       start=(ci == 0), stop=(ci == C8 - 1),
                       waits=w, inc="PE" if ci == C8 - 1 else None)
        ev_prev[i % 2] = em.op("scalar", lambda e, a=ar(g, A_KFM + dc * 2048 + nt * 512, 512), b=pb:
                               e.copy(a, b), waits=[tk_pe], inc="ACT")
    kproj_pe = em.tok("PE")

    t_wv = load_w("wv", [kproj_pe])
    for i, (tc, dh) in enumerate([(t, d) for t in range(16) for d in range(2)]):
        pb = pslots[i % 2][:, :]
        for ci in range(C8):
            w = []
            if ci == 0:
                w = [t_wv if i == 0 else None, ev_prev[i % 2]]
            tk_pe = mm(pb, XN[:, ci * T + tc * 128: ci * T + tc * 128 + 128],
                       WB(ci * 1024 + dh * 512, 512),
                       start=(ci == 0), stop=(ci == C8 - 1),
                       waits=w, inc="PE" if ci == C8 - 1 else None)
        ev_prev[i % 2] = em.op("scalar", lambda e, a=ar(g, A_VTM + tc * 1024 + dh * 512, 512), b=pb:
                               e.copy(a, b), waits=[tk_pe], inc="ACT")
    vproj_pe = em.tok("PE")
    kv_ev = em.tok("ACT")

    t_wq = load_w("wq", [vproj_pe])
    prev_grp_evac = None
    prev_tr_evacs = [kv_ev]
    prev_scores_pe = None
    prev_tr_pe = None
    exp_toks = []
    for grp in range(8):
        tr_evacs_grp = []
        for qs in range(2):
            qt = grp * 2 + qs
            for dc in range(C8):
                for ci in range(C8):
                    w = []
                    if dc == 0 and ci == 0:
                        w = [t_wq if qt == 0 else None] + prev_tr_evacs
                    tk_pe = mm(g.ps_acc[:, dc * 128: dc * 128 + 128],
                               WB(ci * 1024 + dc * 128, 128),
                               XN[:, ci * T + qt * 128: ci * T + qt * 128 + 128],
                               start=(ci == 0), stop=(ci == C8 - 1),
                               waits=w, inc="PE" if (dc == C8 - 1 and ci == C8 - 1) else None)
            qev = em.op("scalar", lambda e, a=ar(g, A_QBF, 1024), b=g.ps_acc[:, :]:
                        e.copy(a, b), waits=[tk_pe, prev_scores_pe], inc="ACT")
            for nk in range(4):
                for dc in range(C8):
                    w = []
                    if nk == 0 and dc == 0:
                        w = [qev, prev_grp_evac]
                    tk_pe = mm(g.ps_big[:, nk * 512: nk * 512 + 512],
                               ar(g, A_QBF + dc * 128, 128),
                               ar(g, A_KFM + dc * 2048 + nk * 512, 512),
                               start=(dc == 0), stop=(dc == C8 - 1),
                               waits=w, inc="PE" if (nk == 3 and dc == C8 - 1) else None)
            prev_scores_pe = tk_pe
            m1 = em.op("vector", lambda e: e.reduce_max(g.sm1[:, 0:1], g.ps_big[:, 0:2048], axis=AX),
                       waits=[tk_pe], inc="DVE")
            m2 = em.op("vector", lambda e: e.tensor_scalar_mul(g.sm1[:, 1:2], g.sm1[:, 0:1], -1.0 / 32.0),
                       waits=[m1], inc="DVE")
            em.wait("vector", [m2])
            ex = em.op("scalar", lambda e: e.activation(
                ar(g, A_ATS, 2048), g.ps_big[:, 0:2048], AF.Exp,
                bias=g.sm1[:, 1:2], scale=1.0 / 32.0, accum_out=g.sm1[:, 3:4]),
                waits=[m2, prev_tr_pe], inc="ACT")
            exp_toks.append(ex)
            rc = em.op("vector", lambda e: e.reciprocal(g.sm1[:, 2:3], g.sm1[:, 3:4]),
                       waits=[ex], inc="DVE")
            sc = em.op("vector", lambda e: e.tensor_scalar_mul(
                ar(g, A_ATS, 2048), ar(g, A_ATS, 2048), g.sm1[:, 2:3]),
                waits=[rc], inc="DVE")
            tr_evacs = []
            for j16 in range(16):
                tk_pe = em.op("tensor", lambda t, o=psb[:, j16 * 128: j16 * 128 + 128],
                              i_=ar(g, A_ATS + j16 * 128, 128):
                              t.transpose(o, i_, g.ident[:]),
                              waits=([sc] + prev_tr_evacs) if j16 == 0 else (),
                              inc="PE" if j16 == 15 else None)
            for half in range(2):
                dst = g.AR[:, A_ATT + half * 8 * 256 + qs * 128:
                           A_ATT + half * 8 * 256 + qs * 128 + 8 * 256] \
                    .rearrange("p (k q) -> p k q", k=8)[:, :, 0:128]
                src = psb[:, half * 1024:(half + 1) * 1024].rearrange("p (k q) -> p k q", k=8)
                if half == 0:
                    tr_evacs.append(em.op("scalar", lambda e, a=dst, bb=src: e.copy(a, bb),
                                          waits=[tk_pe], inc="ACT"))
                else:
                    tr_evacs.append(em.op("vector", lambda e, a=dst, bb=src: e.tensor_copy(a, bb),
                                          waits=[tk_pe], inc="DVE"))
            prev_tr_pe = em.tok("PE")
            prev_tr_evacs = tr_evacs
            tr_evacs_grp += tr_evacs
        for dc in range(C8):
            for kc in range(16):
                w = []
                if kc == 0 and dc == 0:
                    w = tr_evacs_grp + [exp_toks[-1]]
                tk_pe = mm(g.ps_big[:, dc * 256: dc * 256 + 256],
                           ar(g, A_VTM + kc * 1024 + dc * 128, 128),
                           ar(g, A_ATT + kc * 256, 256),
                           start=(kc == 0), stop=(kc == 15),
                           waits=w, inc="PE" if (kc == 15 and dc == C8 - 1) else None)
        for dc in range(C8):
            prev_grp_evac = em.op("vector", lambda e, dc=dc, grp=grp: e.scalar_tensor_tensor(
                X[:, dc * T + grp * 256: dc * T + grp * 256 + 256],
                g.ps_big[:, dc * 256: dc * 256 + 256], 1.0,
                X[:, dc * T + grp * 256: dc * T + grp * 256 + 256], ALU.mult, ALU.add),
                waits=[tk_pe] if dc == 0 else (), inc="DVE")
    g.arena_free = prev_grp_evac
    g.ln_ps_free = prev_grp_evac
    return prev_grp_evac



A1_W, A1_Q, A1_K, A1_V, A1_Y = 0, 24576, 28672, 32768, 36864
A1_KT, A1_KLN, A1_SF, A1_SB, A1_KSQ, A1_NM, A1_MSB, A1_SCT, A1_WO = (
    40960, 41984, 43008, 44032, 44544, 45056, 45568, 46080, 46592)


def emit_l1attn(g, xn_tok):
    em = g.em
    mm = g.mm
    dd = g.d
    X, XN = g.X, g.XN
    psb2 = g.ps_acc[:].bitcast(BF16)
    sf32 = g.AR[:, A1_SF: A1_SF + 1024].bitcast(F32)

    t_w = em.op("gpsimd", lambda e: e.dma_start(
        ar(g, A1_W, 24576).rearrange("p (c o) -> p c o", c=8),
        dd["wqkv"].rearrange("(c p) o -> p c o", p=128)),
        waits=[g.arena_free], inc="LD", inc_n=16)
    wo_ready = [None] * 8

    em.op("vector", lambda e: e.memset(sf32[:, :], 0.0), waits=[g.arena_free], inc="DVE")
    sbf_tok = em.op("vector", lambda e: e.memset(ar(g, A1_SB, 512), 0.0), inc="DVE")

    prev_wo_evac = None
    prev_swo = {}
    kt_free = None
    sup_free = None
    for tile in range(4):
        tw = tile * 512

        def proj_wave(dst_off, builder_lhs, waits0):
            evs = []
            for wv in range(2):
                for m4 in range(4):
                    for ci in range(C8):
                        w = []
                        if m4 == 0 and ci == 0:
                            w = list(waits0) if wv == 0 else [evs[-1]]
                        lhs, rhs = builder_lhs(wv * 4 + m4, ci)
                        tk_pe = mm(g.ps_big[:, m4 * 512: m4 * 512 + 512], lhs, rhs,
                                   start=(ci == 0), stop=(ci == C8 - 1),
                                   waits=w, inc="PE" if (m4 == 3 and ci == C8 - 1) else None)
                evs.append(em.op("vector", lambda e, a=ar(g, dst_off + wv * 2048, 2048),
                                 b=g.ps_big[:, 0:2048]: e.tensor_copy(a, b),
                                 waits=[tk_pe], inc="DVE"))
            return evs[-1]

        def q_lhs(m, ci, osec=0):
            return (ar(g, A1_W + ci * 3072 + osec * 1024 + m * 128, 128),
                    XN[:, ci * T + tw: ci * T + tw + 512])

        def v_lhs(m, ci):
            tc, dh = m // 2, m % 2
            return (XN[:, ci * T + tw + tc * 128: ci * T + tw + tc * 128 + 128],
                    ar(g, A1_W + ci * 3072 + 2048 + dh * 512, 512))

        w0 = [t_w, xn_tok, prev_wo_evac, sbf_tok]
        q_ev = proj_wave(A1_Q, lambda m, ci: q_lhs(m, ci, 0), w0)
        k_ev = proj_wave(A1_K, lambda m, ci: q_lhs(m, ci, 1), [q_ev])
        v_ev = proj_wave(A1_V, v_lhs, [k_ev])

        if DBG_L1A and tile == 0 and not g.dbg_done2:
            em.op("gpsimd", lambda e, a=g.AR[:, A1_K: A1_K + 2048]:
                  e.dma_start(g.d["dbg"][:, 4096:6144], a), waits=[k_ev], inc="LD", inc_n=16)
            em.op("gpsimd", lambda e, a=g.hsel16[:]:
                  e.dma_start(g.d["dbg"][:, 6144:6272], a), waits=[k_ev], inc="LD", inc_n=16)
            g.dbg_done2 = True
        sq_ev = {}
        for cc in range(C8):
            tkq = em.op("vector", lambda e, a=ar(g, A1_KSQ + (cc % 2) * 512, 512),
                        b=ar(g, A1_K + cc * 512, 512): e.tensor_mul(a, b, b),
                        waits=[k_ev] + ([sq_ev[cc - 2]] if cc >= 2 else []), inc="DVE")
            mm(g.ps_s1[0:16, 0:512], g.hsel16[:, cc * 16: cc * 16 + 16],
               ar(g, A1_K + cc * 512, 512), start=(cc == 0), stop=(cc == C8 - 1),
               waits=[k_ev, sup_free] if cc == 0 else ())
            sq_ev[cc] = mm(g.ps_s2[0:16, 0:512], g.hsel16[:, cc * 16: cc * 16 + 16],
                           ar(g, A1_KSQ + (cc % 2) * 512, 512),
                           start=(cc == 0), stop=(cc == C8 - 1),
                           waits=[tkq, sup_free] if cc == 0 else [tkq], inc="PE")
        st_pe = em.tok("PE")
        inv64 = 1.0 / 64.0
        em.op("vector", lambda e: e.tensor_scalar_mul(g.ps_s1[0:16, 0:512], g.ps_s1[0:16, 0:512], inv64),
              waits=[st_pe], inc="DVE")
        em.op("vector", lambda e: e.tensor_scalar_mul(g.ps_s2[0:16, 0:512], g.ps_s2[0:16, 0:512], inv64),
              waits=[em.tok("DVE")], inc="DVE")
        em.op("scalar", lambda e: e.copy(g.AR[0:16, A1_MSB: A1_MSB + 512], g.ps_s1[0:16, 0:512]),
              waits=[em.tok("DVE")], inc="ACT")
        em.op("vector", lambda e: e.scalar_tensor_tensor(
            g.AR[0:16, A1_NM: A1_NM + 512], g.AR[0:16, A1_MSB: A1_MSB + 512], -1.0,
            g.AR[0:16, A1_MSB: A1_MSB + 512], ALU.mult, ALU.mult),
            waits=[em.tok("ACT")], inc="DVE")
        em.op("vector", lambda e: e.tensor_add(g.ps_s2[0:16, 0:512], g.ps_s2[0:16, 0:512],
                                               g.AR[0:16, A1_NM: A1_NM + 512]),
              waits=[em.tok("DVE")], inc="DVE")
        em.op("vector", lambda e: e.tensor_scalar_add(g.ps_s2[0:16, 0:512], g.ps_s2[0:16, 0:512], 1e-5),
              waits=[em.tok("DVE")], inc="DVE")
        s6 = em.op("vector", lambda e: e.reciprocal(g.ps_s2[0:16, 0:512], g.ps_s2[0:16, 0:512]),
                   waits=[em.tok("DVE")], inc="DVE")
        rsb = em.op("scalar", lambda e: e.activation(g.AR[0:16, A1_KLN: A1_KLN + 512],
                                                     g.ps_s2[0:16, 0:512], AF.Sqrt),
                    waits=[s6], inc="ACT")
        mrs = em.op("vector", lambda e: e.tensor_mul(g.AR[0:16, A1_KLN + 512: A1_KLN + 1024],
                                                     g.ps_s1[0:16, 0:512],
                                                     g.AR[0:16, A1_KLN: A1_KLN + 512]),
                    waits=[rsb], inc="DVE")
        for half in range(2):
            for cc4 in range(4):
                cc = half * 4 + cc4
                bcast_pe = mm(g.ps_big[:, cc4 * 512: cc4 * 512 + 512],
                              g.hselT[:, cc * 128: cc * 128 + 128],
                              g.AR[0:16, A1_KLN: A1_KLN + 512],
                              start=True, stop=True,
                              waits=[mrs, em.tok("DVE")] if cc4 == 0 else (),
                              inc="PE" if cc4 == 3 else None)
            t_mul = em.op("vector", lambda e, a=ar(g, A1_K + half * 2048, 2048),
                          b=g.ps_big[:, 0:2048]: e.tensor_mul(a, a, b),
                          waits=[bcast_pe], inc="DVE")
            for cc4 in range(4):
                cc = half * 4 + cc4
                bcast_pe = mm(g.ps_big[:, cc4 * 512: cc4 * 512 + 512],
                              g.hselT[:, cc * 128: cc * 128 + 128],
                              g.AR[0:16, A1_KLN + 512: A1_KLN + 1024],
                              start=True, stop=True,
                              waits=[t_mul] if cc4 == 0 else (),
                              inc="PE" if cc4 == 3 else None)
            em.op("vector", lambda e, a=ar(g, A1_K + half * 2048, 2048),
                  b=g.ps_big[:, 0:2048]: e.tensor_sub(a, a, b),
                  waits=[bcast_pe], inc="DVE")
        kn_ev = em.tok("DVE")
        if DBG_L1A and tile == 0 and not g.dbg_done:
            em.op("gpsimd", lambda e, a=g.AR[:, A1_K: A1_K + 2048]:
                  e.dma_start(g.d["dbg"][:, 0:2048], a), waits=[kn_ev], inc="LD", inc_n=16)
            em.op("gpsimd", lambda e, a=g.AR[0:16, A1_KLN: A1_KLN + 1024]:
                  e.dma_start(g.d["dbg"][0:16, 2048:3072], a), waits=[kn_ev], inc="LD", inc_n=16)
            em.op("gpsimd", lambda e, a=g.AR[0:16, A1_MSB: A1_MSB + 1024]:
                  e.dma_start(g.d["dbg"][0:16, 3072:4096], a), waits=[kn_ev], inc="LD", inc_n=16)
            g.dbg_done = True

        prev_mask = {0: None, 1: None}
        wave_evac = {0: None, 1: prev_wo_evac}
        for tc in range(4):
            for j in range(C8):
                tk_pe = em.op("tensor", lambda t, o=psb2[:, j * 128: j * 128 + 128],
                              i_=g.AR[:, A1_K + j * 512 + tc * 128: A1_K + j * 512 + tc * 128 + 128]:
                              t.transpose(o, i_, g.ident[:]),
                              waits=[kn_ev, v_ev, kt_free] if j == 0 else (),
                              inc="PE" if j == C8 - 1 else None)
            kt_ev = em.op("vector", lambda e, a=ar(g, A1_KT, 1024), b=psb2[:, 0:1024]:
                          e.tensor_copy(a, b), waits=[tk_pe], inc="DVE")
            y_wave_pe = {}
            for h in range(16):
                slot = h % 2
                sc_ps = g.ps_big[:, slot * 512: slot * 512 + 128]
                rbase = 64 * (h % 2)
                cc = h // 2
                kq_off = cc * 512 + tc * 128
                tk_sc = mm(sc_ps,
                           g.AR[rbase:rbase + 64, A1_K + kq_off: A1_K + kq_off + 128],
                           g.AR[rbase:rbase + 64, A1_Q + kq_off: A1_Q + kq_off + 128],
                           start=True, stop=True,
                           waits=[prev_mask[slot]], inc="PE")
                msk = em.op("vector", lambda e,
                            a=g.AR[:, A1_SCT + slot * 128: A1_SCT + slot * 128 + 128],
                            b=sc_ps, c=g.triu128[:]: e.tensor_tensor(a, b, c, ALU.mult),
                            waits=[tk_sc], inc="DVE")
                prev_mask[slot] = msk
                if DBG_L1A and tile == 0 and tc == 0 and h in (0, 2) and not g.dbg_sct.get(h):
                    msk = em.op("gpsimd", lambda e, a=g.AR[:, A1_SCT + slot * 128: A1_SCT + slot * 128 + 128], h=h:
                                e.dma_start(g.d["dbg"][:, 2048 + (h // 2) * 128: 2048 + (h // 2) * 128 + 128], a),
                                waits=[msk], inc="LD", inc_n=16)
                    prev_mask[slot] = msk
                    g.dbg_sct[h] = True
                pair = (h // 2) % 4
                wv = h // 8
                yps = g.ps_s1[rbase:rbase + 64, pair * 128: pair * 128 + 128]
                w_in = [msk]
                if h % 8 == 0:
                    w_in.append(wave_evac[1] if h == 0 else wave_evac[0])
                mm(yps, g.AR[:, A1_V + tc * 1024 + h * 64: A1_V + tc * 1024 + h * 64 + 64],
                   g.AR[:, A1_SCT + slot * 128: A1_SCT + slot * 128 + 128],
                   start=True, stop=False, waits=w_in)
                tk_y = mm(yps, g.AR[rbase:rbase + 64, A1_SB + cc * 64: A1_SB + cc * 64 + 64],
                          g.AR[rbase:rbase + 64, A1_Q + kq_off: A1_Q + kq_off + 128],
                          start=False, stop=True,
                          inc="PE" if h % 8 == 7 else None)
                if h % 8 == 7:
                    y_wave_pe[wv] = tk_y
                    dst = g.AR[:, A1_Y + wv * 2048: A1_Y + wv * 2048 + 2048] \
                        .rearrange("p (dc q) -> p dc q", dc=4)[:, :, tc * 128: tc * 128 + 128]
                    src = g.ps_s1[:, 0:512].rearrange("p (dc q) -> p dc q", dc=4)
                    wave_evac[wv] = em.op("scalar", lambda e, a=dst, b=src: e.copy(a, b),
                                          waits=[tk_y], inc="ACT")
            for h in range(16):
                rbase = 64 * (h % 2)
                tk_up = mm(g.ps_s2[rbase:rbase + 64, (h // 2) * 64: (h // 2) * 64 + 64],
                           g.AR[:, A1_KT + h * 64: A1_KT + h * 64 + 64],
                           g.AR[:, A1_V + tc * 1024 + h * 64: A1_V + tc * 1024 + h * 64 + 64],
                           start=True, stop=True,
                           waits=[sup_free, kt_ev] if h == 0 else (),
                           inc="PE" if h == 15 else None)
            sadd = em.op("vector", lambda e: e.tensor_add(sf32[:, :], sf32[:, :], g.ps_s2[:, :]),
                         waits=[tk_up], inc="DVE")
            sup_free = sadd
            sbf_tok = em.op("vector", lambda e, a=ar(g, A1_SB, 512): e.tensor_copy(a, sf32[:, :]),
                            waits=[sadd], inc="DVE")
            kt_free = sbf_tok
            em.wait("tensor", [sbf_tok])
        y_done = wave_evac[1]
        if DBG_L1A and tile == 0 and not g.dbg_done3:
            em.op("gpsimd", lambda e, a=g.AR[:, A1_Y: A1_Y + 1920]:
                  e.dma_start(g.d["dbg"][:, 6272:8192], a),
                  waits=[wave_evac[0], wave_evac[1]], inc="LD", inc_n=16)
            g.dbg_done3 = True
        for wv in range(2):
            for k in range(C8):
                slot = ar(g, A1_WO + (k % 2) * D, D)
                lt = em.op("gpsimd", lambda e, a=slot, k=k: e.dma_start(
                    a, dd["wo"][k * 128:(k + 1) * 128, :]),
                    waits=[wo_ready[k], prev_swo.get(k % 2), xn_tok],
                    inc="LD", inc_n=16)
                if DBG_L1A and tile == 0 and wv == 0 and k == 0 and not g.dbg_done7:
                    d7 = em.op("gpsimd", lambda e, a=slot[:, 0:512]: e.dma_start(
                        g.d["dbg"][:, 1536:2048], a), waits=[lt], inc="LD", inc_n=16)
                    d8 = em.op("gpsimd", lambda e, a=ar(g, A1_Y, 512): e.dma_start(
                        g.d["dbg"][:, 2304:2816], a), waits=[lt], inc="LD", inc_n=16)
                    y_done = d8 if False else y_done
                    g.dbg_done7 = (d7, d8)
                for m4 in range(4):
                    m = wv * 4 + m4
                    w = []
                    if k == 0 and m4 == 0:
                        w = [y_done, wave_evac[0], lt, prev_wo_evac]
                        if DBG_L1A and g.dbg_done7 and tile == 0:
                            w += list(g.dbg_done7)
                    elif m4 == 0:
                        w = [lt]
                    tk_pe = mm(g.ps_big[:, m4 * 512: m4 * 512 + 512],
                               slot[:, m * 128: m * 128 + 128],
                               ar(g, A1_Y + k * 512, 512),
                               start=(k == 0), stop=(k == C8 - 1),
                               waits=w, inc="PE" if m4 == 3 else None)
                prev_swo[k % 2] = tk_pe
            if DBG_L1A and tile == 0 and wv == 0 and not g.dbg_done6:
                dtk = em.op("scalar", lambda e: e.copy(ar(g, A1_KSQ, 512), g.ps_big[:, 0:512]),
                            waits=[em.tok("PE")], inc="ACT")
                em.op("gpsimd", lambda e: e.dma_start(g.d["dbg"][:, 1024:1536], ar(g, A1_KSQ, 512)),
                      waits=[dtk], inc="LD", inc_n=16)
                em.wait("vector", [dtk])
                em.wait("tensor", [dtk])
                g.dbg_done6 = True
            for m4 in range(4):
                m = wv * 4 + m4
                prev_wo_evac = em.op("vector", lambda e, m=m, m4=m4, tw=tw: e.scalar_tensor_tensor(
                    X[:, m * T + tw: m * T + tw + 512],
                    g.ps_big[:, m4 * 512: m4 * 512 + 512], 1.0,
                    X[:, m * T + tw: m * T + tw + 512], ALU.mult, ALU.add),
                    waits=[em.tok("PE")] if m4 == 0 else (), inc="DVE")
            if DBG_BAR:
                em.barrier()
        if DBG_L1A and tile == 0 and not g.dbg_done5:
            em.op("gpsimd", lambda e: e.dma_start(g.d["dbg"][:, 3072:4096], dd["woc"][0]),
                  waits=[wo_ready[0]], inc="LD", inc_n=16)
            g.dbg_done5 = True
        if DBG_L1A and tile == 3 and not g.dbg_done8:
            em.op("gpsimd", lambda e, a=g.X[:, 0:512].bitcast(BF16):
                  e.dma_start(g.d["dbg"].bitcast(BF16)[:, 4096:5120], a),
                  waits=[prev_wo_evac], inc="LD", inc_n=16)
            g.dbg_done8 = True
        if DBG_L1A and tile == 0 and not g.dbg_done4:
            em.op("gpsimd", lambda e, a=g.X[:, 0:512].bitcast(BF16):
                  e.dma_start(g.d["dbg"].bitcast(BF16)[:, 0:1024], a),
                  waits=[prev_wo_evac], inc="LD", inc_n=16)
            g.dbg_done4 = True
    g.arena_free = prev_wo_evac
    g.ln_ps_free = prev_wo_evac
    return prev_wo_evac



A2_W1, A2_W2, A2_XG, A2_XGF, A2_H, A2_YF = 0, 4096, 8192, 9216, 10240, 10752
A2_YTM, A2_XT, A2_YT, A2_B1R, A2_B2R = 11776, 12800, 13824, 14848, 15360
A2_MSK, A2_TMP, A2_SLOT, A2_MSKF = 18432, 19456, 19968, 20480


def emit_l1ffn(g, xn_tok):
    em = g.em
    mm = g.mm
    dd = g.d
    X, XN = g.X, g.XN
    g.l1y_free = None
    g.l1yt_free = None
    psb2 = g.ps_acc[:].bitcast(BF16)
    tmpf = g.AR[:, A2_TMP: A2_TMP + 512].bitcast(F32)
    sloti = g.AR[:, A2_SLOT: A2_SLOT + 64].bitcast(I32)

    if DBG_L1F:
        em.op("gpsimd", lambda e: e.dma_start(g.d["dbg"][:, 1536:1792], g.routert[:]),
              waits=[xn_tok], inc="LD", inc_n=16)
        em.op("gpsimd", lambda e, a=g.X[:, 0:1024].bitcast(BF16):
              e.dma_start(g.d["dbg"].bitcast(BF16)[:, 8192:10240], a),
              waits=[xn_tok], inc="LD", inc_n=16)
        em.op("gpsimd", lambda e, a=g.XN[:, 0:1024]:
              e.dma_start(g.d["dbg"][:, 5120:6144], a),
              waits=[xn_tok], inc="LD", inc_n=16)
        for ci in range(8):
            em.op("gpsimd", lambda e, ci=ci: e.dma_start(
                g.d["dbg"][:, 2048 + ci * 128: 2048 + ci * 128 + 128],
                g.XN[:, ci * T: ci * T + 128]), waits=[xn_tok], inc="LD", inc_n=16)
    lg_slots = [g.ps_s1, g.ps_s2]
    msk_toks = []
    prev_msk = {0: g.arena_free, 1: g.arena_free}
    for tc in range(16):
        pb = lg_slots[tc % 2][:, 0:32]
        for ci in range(C8):
            mm(pb, XN[:, ci * T + tc * 128: ci * T + tc * 128 + 128],
               g.routert[:, ci * 32: ci * 32 + 32],
               start=(ci == 0), stop=False,
               waits=[xn_tok, prev_msk[tc % 2]] if ci == 0 else ())
        tk_pe = mm(pb, g.ones_col[:], g.rbrow[:], start=False, stop=True, inc="PE")
        if DBG_L1F and tc == 0:
            dtk0 = em.op("scalar", lambda e, c=pb: e.copy(tmpf[:, 224:256], c),
                         waits=[tk_pe], inc="ACT")
            dmp0 = em.op("gpsimd", lambda e: e.dma_start(
                g.d["dbg"][:, 1090:1122], tmpf[:, 224:256]), waits=[dtk0], inc="LD", inc_n=16)
            em.wait("vector", [dmp0])
        m1 = em.op("vector", lambda e, a=tmpf[:, 96:97], b=pb: e.reduce_max(a, b, axis=AX),
                   waits=[tk_pe, em.tok("DVE")], inc="DVE")
        if DBG_L1F and tc == 0:
            em.op("gpsimd", lambda e: e.dma_start(
                g.d["dbg"][:, 1122:1124], tmpf[:, 96:98]), waits=[m1], inc="LD", inc_n=16)
        m2 = em.op("vector", lambda e, a=tmpf[:, 0:32], b=pb, c=tmpf[:, 96:97]:
                   e.tensor_single_scalar(a, b, c, ALU.is_ge), waits=[m1], inc="DVE")
        em.op("vector", lambda e, a=tmpf[:, 32:64], b=tmpf[:, 0:32]:
              e.tensor_tensor_scan(a, b, b, 0.0, ALU.add, ALU.bypass),
              waits=[em.tok("DVE")], inc="DVE")
        em.op("vector", lambda e, a=tmpf[:, 64:96], b=tmpf[:, 32:64]:
              e.tensor_single_scalar(a, b, 1.0, ALU.is_equal),
              waits=[em.tok("DVE")], inc="DVE")
        mskf = g.AR[:, A2_MSKF: A2_MSKF + 1024].bitcast(F32)
        em.op("vector", lambda e, a=mskf[:, tc * 32: tc * 32 + 32],
              b=tmpf[:, 0:32], c=tmpf[:, 64:96]:
              e.tensor_tensor(a, b, c, ALU.mult),
              waits=[em.tok("DVE")], inc="DVE")
        tk = em.op("vector", lambda e, a=ar(g, A2_MSK + tc * 32, 32),
                   b=mskf[:, tc * 32: tc * 32 + 32]:
                   e.tensor_copy(a, b), waits=[em.tok("DVE")], inc="DVE")
        if DBG_L1F and tc == 0:
            em.op("gpsimd", lambda e: e.dma_start(
                g.d["dbg"][:, 1152:1248], tmpf[:, 0:96]), waits=[tk], inc="LD", inc_n=16)
            em.op("gpsimd", lambda e, m=mskf: e.dma_start(
                g.d["dbg"][:, 1248:1280], m[:, 0:32]), waits=[tk], inc="LD", inc_n=16)
            em.op("gpsimd", lambda e: e.dma_start(
                g.d["dbg"][:, 1280:1312], g.AR[:, A2_MSK: A2_MSK + 32]), waits=[tk], inc="LD", inc_n=16)
            em.wait("vector", [em.tok("LD")])
        prev_msk[tc % 2] = tk
        msk_toks.append(tk)

    slot_toks = []
    prev_cum = {0: None, 1: None}
    for tc in range(16):
        pb = lg_slots[tc % 2][:, 64:96]
        for sc in range(tc + 1):
            lhs = g.triu128[:] if sc == tc else g.ones128[:]
            mm(pb, lhs, ar(g, A2_MSK + sc * 32, 32),
               start=(sc == 0), stop=False,
               waits=[msk_toks[-1], prev_cum[tc % 2]] if sc == 0 else ())
        tk_pe = mm(pb, g.ones_col[:], g.eoffrow[:], start=False, stop=True, inc="PE")
        if DBG_L1F and tc == 0:
            dtk = em.op("scalar", lambda e, c=pb: e.copy(tmpf[:, 192:224], c),
                        waits=[tk_pe], inc="ACT")
            em.op("gpsimd", lambda e: e.dma_start(
                g.d["dbg"][:, 1024:1056], tmpf[:, 192:224]), waits=[dtk], inc="LD", inc_n=16)
        mskf = g.AR[:, A2_MSKF: A2_MSKF + 1024].bitcast(F32)
        a1 = em.op("vector", lambda e, a=tmpf[:, 128:160], b=mskf[:, tc * 32: tc * 32 + 32], c=pb:
                   e.tensor_tensor(a, c, b, ALU.mult), waits=[tk_pe, em.tok("DVE")], inc="DVE")
        if DBG_L1F and tc == 0:
            em.op("gpsimd", lambda e: e.dma_start(
                g.d["dbg"][:, 1056:1088], tmpf[:, 128:160]), waits=[a1], inc="LD", inc_n=16)
        a2 = em.op("vector", lambda e, a=tmpf[:, 160:161], b=tmpf[:, 128:160]:
                   e.reduce_sum(a, b, axis=AX), waits=[a1], inc="DVE")
        prev_cum[tc % 2] = a2
        a3 = em.op("vector", lambda e, a=tmpf[:, 160:161]:
                   e.tensor_scalar_add(a, a, -1.0), waits=[a2], inc="DVE")
        a4 = em.op("vector", lambda e, a=tmpf[:, 161:162], b=tmpf[:, 160:161]:
                   e.tensor_single_scalar(a, b, 0.0, ALU.is_lt), waits=[a3], inc="DVE")
        a5 = em.op("vector", lambda e, a=tmpf[:, 160:161], b=tmpf[:, 161:162]:
                   e.scalar_tensor_tensor(a, b, 4097.0, a, ALU.mult, ALU.add),
                   waits=[a4], inc="DVE")
        a5 = em.op("vector", lambda e, a=tmpf[:, 160:161]:
                   e.tensor_scalar_min(a, a, 4096.0), waits=[a5], inc="DVE")
        if DBG_L1F and tc == 0:
            em.op("gpsimd", lambda e: e.dma_start(
                g.d["dbg"][:, 1088:1090], tmpf[:, 160:162]), waits=[a5], inc="LD", inc_n=16)
        em.op("vector", lambda e, a=sloti[:, tc:tc + 1], b=tmpf[:, 160:161]:
              e.tensor_copy(a, b), waits=[a5], inc="DVE")
        slot_toks.append(em.op("vector", lambda e, a=tmpf[:, 162:163], b=sloti[:, tc:tc + 1]:
                               e.tensor_copy(a, b), waits=[em.tok("DVE")], inc="DVE"))

    if DBG_L1F:
        em.op("gpsimd", lambda e: e.dma_start(
            g.d["dbg"].bitcast(BF16)[:, 0:64], g.AR[:, A2_SLOT: A2_SLOT + 64]),
            waits=[slot_toks[-1]], inc="LD", inc_n=16)
        em.op("gpsimd", lambda e: e.dma_start(
            g.d["dbg"][:, 512:1024], ar(g, A2_MSK, 512)),
            waits=[slot_toks[-1]], inc="LD", inc_n=16)

    prev_ph2 = None
    scat_toks = []
    for tc in (range(0) if SKIP_L1F_P23 else range(16)):
        for j in range(C8):
            tk_pe = em.op("tensor", lambda t, o=psb2[:, j * 128: j * 128 + 128],
                          i_=XN[:, j * T + tc * 128: j * T + tc * 128 + 128]:
                          t.transpose(o, i_, g.ident[:]),
                          waits=[slot_toks[tc], prev_ph2] if j == 0 else (),
                          inc="PE" if j == C8 - 1 else None)
        ev = em.op("vector", lambda e, a=ar(g, A2_XT, 1024), b=psb2[:, 0:1024]:
                   e.tensor_copy(a, b), waits=[tk_pe], inc="DVE")
        sc = em.op("gpsimd", lambda e, i=sloti[:, tc:tc + 1]: e.indirect_dma_start(
            dd["xg_dram"][0:128, :], IndirectOffsetOnAxis(i, 0),
            ar(g, A2_XT, 1024), None),
            waits=[ev], inc="LD", inc_n=16)
        prev_ph2 = sc
        scat_toks.append(sc)

    slot_pe = {0: None, 1: None}
    slot_ld = {0: None, 1: None}
    store_toks = []
    prev_gather = None
    prev_ytr = None
    for e in (range(0) if SKIP_L1F_P23 else range(32)):
        s = e % 2
        lw1 = em.op("gpsimd", lambda e_, x=e, s=s: e_.dma_start(
            ar(g, A2_W1 + s * 2048, 2048).rearrange("p (c h) -> p c h", c=8),
            dd["w1L1"][x].rearrange("(c p) h -> p c h", p=128)),
            waits=[slot_pe[s]], inc="LD", inc_n=16)
        lw2 = em.op("gpsimd", lambda e_, x=e, s=s: e_.dma_start(
            ar(g, A2_W2 + s * 2048, 2048).rearrange("p (c h) -> p c h", c=2),
            dd["w2L1"][x].rearrange("(c p) h -> p c h", p=128)), inc="LD", inc_n=16)
        lb1 = em.op("gpsimd", lambda e_, x=e, s=s: e_.dma_start(
            g.AR[0:1, A2_B1R + s * 256: A2_B1R + s * 256 + 256], dd["b1L1r"][x:x + 1, :]),
            inc="LD", inc_n=16)
        lb2 = em.op("gpsimd", lambda e_, x=e, s=s: e_.dma_start(
            g.AR[0:1, A2_B2R + s * 1024: A2_B2R + s * 1024 + 1024], dd["b2L1r"][x:x + 1, :]),
            inc="LD", inc_n=16)
        slot_ld[s] = lb2
        gt = em.op("sync", lambda e_, x=e: e_.dma_start(
            ar(g, A2_XG, 1024), dd["xg_dram"][x * 128:(x + 1) * 128, :]),
            waits=[scat_toks[-1], prev_gather], inc="IO", inc_n=16)
        for j in range(C8):
            tk_pe = em.op("tensor", lambda t, o=psb2[:, j * 128: j * 128 + 128],
                          i_=g.AR[:, A2_XG + j * 128: A2_XG + j * 128 + 128]:
                          t.transpose(o, i_, g.ident[:]),
                          waits=[gt, prev_ytr] if j == 0 else (),
                          inc="PE" if j == C8 - 1 else None)
        xgf_ev = em.op("vector", lambda e_, a=ar(g, A2_XGF, 1024), b=psb2[:, 0:1024]:
                       e_.tensor_copy(a, b), waits=[tk_pe], inc="DVE")
        prev_gather = xgf_ev
        h_evs = []
        for mh in range(2):
            hp = [g.ps_s1, g.ps_s2][mh][:, 128:256]
            for ci in range(C8):
                mm(hp, ar(g, A2_W1 + s * 2048 + ci * 256 + mh * 128, 128),
                   ar(g, A2_XGF + ci * 128, 128),
                   start=(ci == 0), stop=False,
                   waits=[lw1, xgf_ev, h_evs[mh - 2] if mh >= 2 else None] if ci == 0 else ())
            tk_pe = mm(hp, g.AR[0:1, A2_B1R + s * 256 + mh * 128: A2_B1R + s * 256 + mh * 128 + 128],
                       g.ones_row[0:1, 0:128], start=False, stop=True,
                       waits=[lb1] if mh == 0 else (), inc="PE")
            h_evs.append(em.op("scalar", lambda e_, a=ar(g, A2_H + mh * 128, 128), b=hp:
                               e_.activation(a, b, AF.Relu), waits=[tk_pe], inc="ACT"))
        for m in range(C8):
            pb = g.ps_big[:, m * 128: m * 128 + 128]
            for kh in range(2):
                mm(pb, ar(g, A2_W2 + s * 2048 + kh * 1024 + m * 128, 128),
                   ar(g, A2_H + kh * 128, 128),
                   start=(kh == 0), stop=False,
                   waits=[h_evs[-1], g.l1y_free, lw2, lb2] if (m == 0 and kh == 0) else ())
            tk_pe = mm(pb, g.AR[0:1, A2_B2R + s * 1024 + m * 128: A2_B2R + s * 1024 + m * 128 + 128],
                       g.ones_row[0:1, 0:128], start=False, stop=True,
                       inc="PE" if m == C8 - 1 else None)
        slot_pe[s] = tk_pe
        yf_ev = em.op("scalar", lambda e_, a=ar(g, A2_YF, 1024), b=g.ps_big[:, 0:1024]:
                      e_.activation(a, b, AF.Relu), waits=[tk_pe], inc="ACT")
        for j in range(C8):
            tk_pe = em.op("tensor", lambda t, o=psb2[:, 1024 + j * 128: 1024 + j * 128 + 128],
                          i_=g.AR[:, A2_YF + j * 128: A2_YF + j * 128 + 128]:
                          t.transpose(o, i_, g.ident[:]),
                          waits=[yf_ev, g.l1yt_free] if j == 0 else (),
                          inc="PE" if j == C8 - 1 else None)
        ytm_ev = em.op("vector", lambda e_, a=ar(g, A2_YTM, 1024), b=psb2[:, 1024:2048]:
                       e_.tensor_copy(a, b), waits=[tk_pe], inc="DVE")
        prev_ytr = ytm_ev
        g.l1y_free = yf_ev
        st = em.op("sync", lambda e_, x=e: e_.dma_start(
            dd["yg_dram"][x * 128:(x + 1) * 128, :], ar(g, A2_YTM, 1024)),
            waits=[ytm_ev], inc="IO", inc_n=16)
        g.l1yt_free = st
        store_toks.append(st)

    prev_add = None
    prev_g4 = None
    for tc in (range(0) if SKIP_L1F_P4 else range(16)):
        gt = em.op("gpsimd", lambda e, i=sloti[:, tc:tc + 1]: e.indirect_dma_start(
            ar(g, A2_YT, 1024), None,
            dd["yg_dram"][0:128, :], IndirectOffsetOnAxis(i, 0)),
            waits=[store_toks[-1], prev_g4], inc="LD", inc_n=16)
        for j in range(C8):
            tk_pe = em.op("tensor", lambda t, o=psb2[:, j * 128: j * 128 + 128],
                          i_=g.AR[:, A2_YT + j * 128: A2_YT + j * 128 + 128]:
                          t.transpose(o, i_, g.ident[:]),
                          waits=[gt, prev_add] if j == 0 else (),
                          inc="PE" if j == C8 - 1 else None)
        xap = X[:].rearrange("p (c t) -> p c t", c=C8)[:, :, tc * 128: tc * 128 + 128]
        prev_add = em.op("vector", lambda e, a=xap, b=psb2[:, 0:1024].rearrange("p (c q) -> p c q", c=8):
                         e.scalar_tensor_tensor(a, b, 0.5, a, ALU.mult, ALU.add),
                         waits=[tk_pe], inc="DVE")
        prev_g4 = prev_add
    if SKIP_L1F_P4:
        prev_add = store_toks[-1] if store_toks else slot_toks[-1]
    g.arena_free = prev_add
    g.ln_ps_free = prev_add
    return prev_add




DBG_L1A = False
DBG_L1F = False
DBG_BAR = False
SKIP_L1F_P4 = False
SKIP_L1F_P23 = False
SW2DIRECT = True


def build_program(parts):
    nc = bass.Bass("TRN2", target_bir_lowering=False)
    g = G()
    g.nc = nc
    g.em = Em()
    g.mm = None
    em = g.em
    for s in ("LD", "IO", "PE", "ACT", "DVE", "BAR"):
        em.sem(s)

    def mm(out_ap, lhsT_ap, rhs_ap, start, stop, waits=(), inc=None):
        return em.op(
            "tensor",
            lambda t, o=out_ap, l=lhsT_ap, r=rhs_ap, s1=start, s2=stop: t.matmul(
                o, l, r, start=s1, stop=s2),
            waits=waits, inc=inc)
    g.mm = mm

    g.d = declare_inputs(nc, parts)
    alloc_state(g, parts)
    g.gv_prev = {}
    g.sw2_cached = False
    g.wo_cached = False
    g.dbg_done = False
    g.dbg_done2 = False
    g.dbg_done3 = False
    g.dbg_sct = {}
    g.dbg_done4 = False
    g.dbg_done5 = False
    g.dbg_done6 = False
    g.dbg_done7 = None
    g.dbg_done8 = False

    c_tok = emit_load_consts(g, parts)
    g.arena_free = None
    g.ln_ps_free = None
    for seq in range(NSEQ):
        x_tok = em.op("sync", lambda e, seq=seq: e.dma_start(
            g.X[:].rearrange("p (c t) -> p c t", c=8), g.d["xT"][seq].rearrange("(c p) t -> p c t", p=128)),
            inc="IO", inc_n=16)
        g.ln_pre = x_tok
        em.wait("vector", [c_tok])
        last = None
        for sl in parts:
            xn_tok = emit_ln(g, g.ln_pre)
            if sl == "l0ffn":
                last = emit_l0ffn(g, xn_tok)
            elif sl == "l0attn":
                last = emit_l0attn(g, xn_tok)
            elif sl == "l1attn":
                last = emit_l1attn(g, xn_tok)
            elif sl == "l1ffn":
                last = emit_l1ffn(g, xn_tok)
            em.barrier()
            g.ln_pre = None
            g.arena_free = None
            g.ln_ps_free = None
        st = em.op("sync", lambda e, seq=seq: e.dma_start(
            g.d["outT"][seq].rearrange("(c p) t -> p c t", p=128), g.X[:].rearrange("p (c t) -> p c t", c=8)),
            inc="IO", inc_n=16)
        em.wait("sync", [st])
        em.barrier()
    em.build(nc)
    return g




def make_consts():
    c = {}
    c["ident"] = np.eye(128, dtype=np.float32)
    c["ones_col"] = np.ones((1, 128), np.float32)
    c["ones_red"] = np.ones((128, 1), np.float32)
    c["ones_row"] = np.ones((128, 512), np.float32)
    return c


def fm(v, nch):
    return np.ascontiguousarray(v.reshape(nch, 128).T)


def prep_inputs(parts, inputs):
    base = make_consts()
    if "l0ffn" in parts:
        base["sw1"] = inputs["L0_ffn_sw1"]
        base["sw3"] = inputs["L0_ffn_sw3"]
        base["sw2"] = inputs["L0_ffn_sw2"]
        base["sb1fm"] = fm(np.asarray(inputs["L0_ffn_sb1"]), 16)
        base["sb3fm"] = fm(np.asarray(inputs["L0_ffn_sb3"]), 16)
        rp = np.zeros((128, 3072), np.float32)
        rp[64, 2048:3072] = np.asarray(inputs["L0_ffn_sb2"])
        b2 = np.asarray(inputs["L0_ffn_b2"])
        for e in range(8):
            rp[32 * (e % 3), (e // 3) * 1024:(e // 3 + 1) * 1024] = b2[e]
        base["rowpack"] = rp
        base["w1"] = inputs["L0_ffn_w1"]
        base["w2"] = inputs["L0_ffn_w2"]
        base["b1fm"] = np.ascontiguousarray(
            np.asarray(inputs["L0_ffn_b1"]).reshape(8, 8, 128).transpose(2, 0, 1).reshape(128, 64))
    if "l0attn" in parts:
        base["wq"] = inputs["L0_attn_Wq"]; base["wk"] = inputs["L0_attn_Wk"]
        base["wv"] = inputs["L0_attn_Wv"]
    if "l1attn" in parts:
        base["wqkv"] = inputs["L1_attn_W_qkv"]; base["wo"] = inputs["L1_attn_W_o"]
        hs = np.zeros((128, 128), np.float32)
        ht = np.zeros((16, 1024), np.float32)
        for cc in range(8):
            for h2 in range(2):
                h = 2 * cc + h2
                hs[64 * h2:64 * h2 + 64, cc * 16 + h] = 1.0
                ht[h, cc * 128 + 64 * h2: cc * 128 + 64 * h2 + 64] = 1.0
        base["hsel16"] = hs; base["hselT"] = ht
    if ("l1attn" in parts) or ("l1ffn" in parts):
        base["triu128"] = np.triu(np.ones((128, 128), np.float32))
    if "l1ffn" in parts:
        base["routerT"] = np.ascontiguousarray(np.asarray(inputs["L1_ffn_router_w"]).T)
        base["rbrow"] = np.asarray(inputs["L1_ffn_router_b"]).reshape(1, 32)
        base["w1L1"] = inputs["L1_ffn_weights1"]
        base["w2L1"] = inputs["L1_ffn_weights2"]
        base["b1L1r"] = np.asarray(inputs["L1_ffn_bias1"], dtype=np.float32)
        base["b2L1r"] = np.asarray(inputs["L1_ffn_bias2"], dtype=np.float32)
        base["eoffrow"] = (np.arange(32, dtype=np.float32) * 128).reshape(1, 32)
        base["ones128"] = np.ones((128, 128), np.float32)
    return base


LAST_DBG = None


def run_parts(parts, inputs, x):
    g = build_program(parts)
    base = prep_inputs(parts, inputs)
    xT = np.ascontiguousarray(x.transpose(0, 2, 1)).astype(np.float32)
    in_maps = []
    for c in range(NCORE):
        m = dict(base)
        m["xT"] = np.ascontiguousarray(xT[2 * c: 2 * c + 2])
        in_maps.append(m)
    res = run_bass_kernel_spmd(g.nc, in_maps, core_ids=list(range(NCORE)))
    global LAST_DBG
    LAST_DBG = np.asarray(res.results[0].get("dbg")) if "dbg" in res.results[0] else None
    out = np.concatenate([r["outT"] for r in res.results], axis=0)
    return np.ascontiguousarray(
        out.reshape(NCORE * NSEQ, D, T).transpose(0, 2, 1))


def kernel(**inputs):
    _ = np.asarray(inputs["L0_ffn_t_emb"])
    x = np.asarray(inputs["L0_attn_x"], dtype=np.float32)
    return run_parts(["l0ffn", "l0attn", "l0ffn", "l1ffn", "l1attn", "l1ffn"],
                     inputs, x)
